# version 15
# baseline (speedup 1.0000x reference)
import contextlib
import numpy as np
import ml_dtypes
import concourse.bass as bass
import concourse.mybir as mybir
from concourse.bass_utils import run_bass_kernel_spmd

F32 = mybir.dt.float32
I32 = mybir.dt.int32
BF16 = mybir.dt.bfloat16
AF = mybir.ActivationFunctionType
ALU = mybir.AluOpType
AX = mybir.AxisListType

SEQ, OWN, D, DKV = 2048, 1024, 4096, 1024
FC = D // 128
KC = DKV // 128
NH, NKV, HD = 32, 8, 128
H8, D8, HID = 1366, 512, 10928
NOC = 11
LAST_OC = H8 - 10 * 128
FLAT8 = (8 * H8 * D8) // 128
EPS_LN = 1e-5
EPS_AT = 1e-10
N_CORES = 8


def _oct_tables():
    TR = [(1, 2, 3), (1, 4, 5), (1, 7, 6), (2, 4, 6), (2, 5, 7), (3, 4, 7), (3, 6, 5)]
    S = np.zeros((8, 8, 8), np.float32)
    S[0, 0, 0] = 1.0
    for i in range(1, 8):
        S[0, i, i] = 1.0
        S[i, 0, i] = 1.0
        S[i, i, 0] = -1.0
    for a, b, c in TR:
        for x, y, z in ((a, b, c), (b, c, a), (c, a, b)):
            S[x, y, z] = 1.0
            S[y, x, z] = -1.0
    jof = np.zeros((8, 8), int)
    sgn = np.zeros((8, 8), int)
    for i in range(8):
        for k in range(8):
            js = [j for j in range(8) if S[j, i, k] != 0]
            assert len(js) == 1
            jof[i, k] = js[0]
            sgn[i, k] = int(S[js[0], i, k])
    return jof, sgn


JOF, SGN = _oct_tables()

N_PHASES = 99
DEBUG_TAPS = ()


class Sem:
    __slots__ = ("h", "n")

    def __init__(self, h):
        self.h = h
        self.n = 0

    def inc(self, inst, k=1):
        inst.then_inc(self.h, k)
        self.n += k
        return self.n


_ENG_OPS = (
    "memset", "memzero", "tensor_copy", "tensor_tensor", "tensor_add",
    "tensor_sub", "tensor_mul", "tensor_max", "tensor_scalar",
    "tensor_single_scalar", "tensor_scalar_add", "tensor_scalar_sub",
    "tensor_scalar_mul", "tensor_scalar_max", "tensor_scalar_min",
    "tensor_relu", "tensor_reduce", "reduce_sum", "reduce_max",
    "scalar_tensor_tensor", "reciprocal", "activation", "sign", "square",
    "bn_stats", "bn_aggr", "select", "transpose",
    "copy", "add", "mul",
)


class Eng:

    def __init__(self, c, e, sem, qsem=None):
        self.c = c
        self.e = e
        self.s = sem
        self.q = qsem

    def wait(self, sem, val):
        self.e.wait_ge(sem.h, val)

    def dma(self, q, out, in_, **kw):
        if q.n:
            self.e.wait_ge(q.h, q.n)
        ins = self.e.dma_start(out=out, in_=in_, **kw)
        q.inc(ins, 16)
        return q.n

    def nop_inc(self):
        self.s.inc(self.e.nop())
        return self.s.n

    def __getattr__(self, name):
        if name not in _ENG_OPS:
            raise AttributeError(name)
        fn = getattr(self.e, name)

        def call(*a, dep=True, **kw):
            if dep:
                self.e.wait_ge(self.s.h, self.s.n)
            ins = fn(*a, **kw)
            self.s.inc(ins)
            return self.s.n
        return call


class Ctx:
    def __init__(self, nc, es):
        self.nc = nc
        self.es = es
        self.d = {}
        self.t = {}

    def sem(self, name):
        s = Sem(self.es.enter_context(self.nc.semaphore(name)))
        setattr(self, name, s)
        return s

    def sbuf(self, es, name, shape, dt):
        return es.enter_context(self.nc.sbuf_tensor(name, shape, dt))

    def psum(self, es, name, shape, dt=F32):
        return es.enter_context(self.nc.psum_tensor(name, shape, dt))

    def dram(self, name, shape, dt, kind="Internal"):
        if kind == "Internal" and name in DEBUG_TAPS:
            kind = "ExternalOutput"
        ap = self.nc.dram_tensor(name, shape, dt, kind=kind).ap()
        self.d[name] = ap
        return ap

    def sync_body(self, blk, fn):
        blk.sync(lambda e: fn(Eng(self, e, self.pe)))

    def scalarq_body(self, blk, fn):
        blk.scalar(lambda e: fn(Eng(self, e, self.ac)))

    act_body = scalarq_body

    def dve_body(self, blk, fn):
        blk.vector(lambda e: fn(Eng(self, e, self.dv)))

    def pool_body(self, blk, fn):
        blk.gpsimd(lambda e: fn(Eng(self, e, self.gp)))

    def pe_body(self, blk, fn):
        blk.tensor(lambda e: fn(Eng(self, e, self.pe)))


def _rsqrt_dve(E, c, stat2):
    vi = stat2[:, 50:51].bitcast(I32)
    yi = stat2[:, 51:52].bitcast(I32)
    ti = stat2[:, 52:53].bitcast(I32)
    E.tensor_scalar(out=ti, in0=vi, scalar1=1, scalar2=None,
                    op0=ALU.arith_shift_right)
    E.tensor_tensor(out=yi, in0=c.t["rsq_m"][:], in1=ti, op=ALU.subtract)
    for _ in range(3):
        E.tensor_tensor(out=stat2[:, 52:53], in0=stat2[:, 51:52],
                        in1=stat2[:, 51:52], op=ALU.mult)
        E.tensor_tensor(out=stat2[:, 52:53], in0=stat2[:, 52:53],
                        in1=stat2[:, 50:51], op=ALU.mult)
        E.tensor_scalar(out=stat2[:, 52:53], in0=stat2[:, 52:53],
                        scalar1=-0.5, scalar2=1.5, op0=ALU.mult, op1=ALU.add)
        E.tensor_tensor(out=stat2[:, 51:52], in0=stat2[:, 51:52],
                        in1=stat2[:, 52:53], op=ALU.mult)


def barrier(c):
    nc = c.nc
    target = c.bar.n + 5
    with nc.Block() as blk:
        def b_sync(e):
            for q in c.q + c.qsw:
                if q.n:
                    e.wait_ge(q.h, q.n)
            e.nop().then_inc(c.bar.h, 1)
            e.wait_ge(c.bar.h, target)

        def body(e):
            e.nop().then_inc(c.bar.h, 1)
            e.wait_ge(c.bar.h, target)
        blk.sync(b_sync)
        blk.scalar(body)
        blk.vector(body)
        blk.tensor(body)
        blk.gpsimd(body)
    c.bar.n = target


def declare_io(c):
    d = c.dram
    d("x", [SEQ, D], F32, kind="ExternalInput")
    d("Wq", [D, D], F32, kind="ExternalInput")
    d("Wk", [DKV, D], F32, kind="ExternalInput")
    d("Wv", [DKV, D], F32, kind="ExternalInput")
    d("Wo", [D, D], F32, kind="ExternalInput")
    d("bq", [D], F32, kind="ExternalInput")
    d("bk", [DKV], F32, kind="ExternalInput")
    d("bv", [DKV], F32, kind="ExternalInput")
    d("bo", [D], F32, kind="ExternalInput")
    d("w_gate", [8, H8, D8], F32, kind="ExternalInput")
    d("w_up", [8, H8, D8], F32, kind="ExternalInput")
    d("w_down", [8, D8, H8], F32, kind="ExternalInput")
    d("ident", [128, 128], F32, kind="ExternalInput")
    d("out", [OWN, D], F32, kind="ExternalOutput")
    d("wqT_hi", [FC, 128, FC, 128], BF16)
    d("wqT_lo", [FC, 128, FC, 128], BF16)
    d("wkT_hi", [KC, 128, FC, 128], BF16)
    d("wkT_lo", [KC, 128, FC, 128], BF16)
    d("wvT_hi", [KC, 128, FC, 128], BF16)
    d("wvT_lo", [KC, 128, FC, 128], BF16)
    d("woT_hi", [FC, 128, FC, 128], BF16)
    d("woT_lo", [FC, 128, FC, 128], BF16)
    d("ln1fm_hi", [2, 128, FC * 512], BF16)
    d("ln1fm_lo", [2, 128, FC * 512], BF16)
    d("o_tm", [8, FC, 128, 128], F32)
    d("ffn_tm", [8, FC, 128, 128], F32)
    d("x1", [OWN, D], F32)
    d("ln3fm_hi", [2, 128, 4, FC, 128], BF16)
    d("ln3fm_lo", [2, 128, 4, FC, 128], BF16)
    d("sg", [2, 88, 128, 512], F32)
    d("h_hi", [2, 88, 128, 512], BF16)
    d("h_lo", [2, 88, 128, 512], BF16)



def phase_init(c):
    nc = c.nc
    es = c.es
    t = c.t
    t["id_f32"] = c.sbuf(es, "id_f32", [128, 128], F32)
    t["id_bf"] = c.sbuf(es, "id_bf", [128, 128], BF16)
    t["bias_cols"] = c.sbuf(es, "bias_cols", [128, 80], F32)
    t["nbias_cols"] = c.sbuf(es, "nbias_cols", [128, 80], F32)
    t["scale_half"] = c.sbuf(es, "scale_half", [128, 8], F32)
    t["kvt_hi"] = c.sbuf(es, "kvt_hi", [128, DKV], BF16)
    t["kvt_lo"] = c.sbuf(es, "kvt_lo", [128, DKV], BF16)
    t["recip"] = c.sbuf(es, "recip", [128, NKV], F32)
    t["scol"] = c.sbuf(es, "scol", [128, NKV], F32)
    t["ones11"] = c.sbuf(es, "ones11", [1, 1], F32)
    t["rsq_m"] = c.sbuf(es, "rsq_m", [128, 1], I32)

    ps_init = contextlib.ExitStack()
    zpad = c.sbuf(ps_init, "zpad", [42, 16 * 512], BF16)
    brow = c.sbuf(ps_init, "brow", [1, 4 * D + 128], F32)
    red = c.sbuf(ps_init, "red", [128, 5], F32)
    wbuf = [c.sbuf(ps_init, f"sc_wbuf{i}", [128, 4096], F32) for i in range(2)]
    tots = c.sbuf(ps_init, "tots", [1, 4], F32)
    ps_bias = c.psum(ps_init, "ps_bias", [128, 512], F32)
    ps_misc = c.psum(ps_init, "ps_misc", [128, 512], F32)

    W_SPECS = [("bq", D, 0), ("bk", DKV, 32), ("bv", DKV, 40), ("bo", D, 48)]
    n_ck = 11
    G = 3 * n_ck
    dv0, ac0, gp0, pe0 = c.dv.n, c.ac.n, c.gp.n, c.pe.n

    with nc.Block() as blk:
        def b_pool(E):
            E.memset(red[:], 0.0)
            E.memset(red[:, 4:5], 1.0)
            E.memset(t["ones11"][:], 1.0)
            E.memset(brow[0:1, 4 * D:4 * D + 128], 1.0)
            E.memset(t["scale_half"][:], 0.0)
            E.memset(t["rsq_m"][:], 0x5F3759DF)
            E.memset(zpad[:], 0.0)
        c.pool_body(blk, b_pool)

        q0 = c.q[0]
        q0_base = q0.n

        def b_sync(E):
            E.dma(q0, t["id_f32"][:], c.d["ident"])
            for i, (nm, w, _) in enumerate(W_SPECS):
                E.dma(q0, brow[0:1, i * D:i * D + w],
                      c.d[nm].rearrange("(o n) -> o n", o=1))
            E.wait(c.gp, gp0 + 7)
            for nm in ("h_hi", "h_lo"):
                E.dma(q0, c.d[nm].rearrange(
                    "s (k o) p t -> p s k o t", o=NOC)[LAST_OC:128, :, :,
                                                       NOC - 1, :],
                    zpad[:].rearrange("p (s k t) -> p s k t", s=2, k=8))
        c.sync_body(blk, b_sync)

        qw = [c.q[1], c.q[2]]
        qw_base = [q.n for q in qw]

        def b_act(E):
            for wi, nm in enumerate(["w_gate", "w_up", "w_down"]):
                flat = c.d[nm].rearrange("a b c -> (a b c)").rearrange(
                    "(p n) -> p n", p=128)
                for k in range(n_ck):
                    lo = k * 4096
                    hi = min(FLAT8, lo + 4096)
                    g = wi * n_ck + k
                    if g >= 2:
                        E.wait(c.dv, dv0 + 2 * (g - 1))
                    E.dma(qw[g % 2], wbuf[g % 2][:, 0:hi - lo], flat[:, lo:hi])
            E.wait(c.pe, pe0 + 2)
            E.activation(out=tots[0:1, 0:3], in_=ps_misc[0:1, 0:3],
                         func=AF.Copy, scale=0.5 / (8 * H8 * D8))
            E.wait(q0, q0_base + 16)
            E.activation(out=t["id_bf"][:], in_=t["id_f32"][:], func=AF.Copy)
            E.wait(c.pe, pe0 + 3)
            E.wait(c.gp, gp0 + 6)
            E.activation(out=t["scale_half"][:, 0:3], in_=ps_misc[:, 8:11],
                         func=AF.Copy)
        c.act_body(blk, b_act)

        def b_vector(E):
            E.wait(c.gp, gp0 + 6)
            for wi in range(3):
                for k in range(n_ck):
                    g = wi * n_ck + k
                    lo = k * 4096
                    hi = min(FLAT8, lo + 4096)
                    E.wait(qw[g % 2], qw_base[g % 2] + 16 * (g // 2 + 1))
                    E.reduce_sum(out=red[:, 3:4], in_=wbuf[g % 2][:, 0:hi - lo],
                                 axis=AX.X, apply_absolute_value=True)
                    E.tensor_tensor(out=red[:, wi:wi + 1], in0=red[:, wi:wi + 1],
                                    in1=red[:, 3:4], op=ALU.add)
            E.wait(c.pe, pe0 + 1)
            E.tensor_copy(out=t["bias_cols"][:], in_=ps_bias[:, 0:80])
            E.tensor_scalar_mul(out=t["nbias_cols"][:], in0=t["bias_cols"][:],
                                scalar1=-1.0)
            E.wait(c.ac, ac0 + 3)
            E.tensor_scalar_mul(out=t["scale_half"][:, 4:7],
                                in0=t["scale_half"][:, 0:3], scalar1=2.0)
        c.dve_body(blk, b_vector)

        def b_tensor(E):
            e = E.e
            E.wait(q0, q0_base + 16 * 5)
            E.wait(c.gp, gp0 + 6)
            n = 0
            for i, (nm, w, base) in enumerate(W_SPECS):
                for ch in range(w // 128):
                    ins = e.matmul(out=ps_bias[:, base + ch:base + ch + 1],
                                   lhsT=brow[0:1, i * D + ch * 128:i * D + ch * 128 + 128],
                                   rhs=t["ones11"][:],
                                   start=(n == 0), stop=(n == 79))
                    n += 1
            c.pe.inc(ins)
            E.wait(c.dv, dv0 + 2 * G)
            for wi in range(3):
                ins = e.matmul(out=ps_misc[0:1, wi:wi + 1],
                               lhsT=red[:, wi:wi + 1], rhs=red[:, 4:5],
                               start=(wi == 0), stop=(wi == 2))
            c.pe.inc(ins)
            E.wait(c.ac, ac0 + 1)
            for wi in range(3):
                ins = e.matmul(out=ps_misc[:, 8 + wi:9 + wi],
                               lhsT=brow[0:1, 4 * D:4 * D + 128],
                               rhs=tots[0:1, wi:wi + 1],
                               start=(wi == 0), stop=(wi == 2))
            c.pe.inc(ins)
        c.pe_body(blk, b_tensor)

    assert c.dv.n == dv0 + 2 * G + 3
    assert c.pe.n == pe0 + 3
    assert c.ac.n == ac0 + 3
    ps_init.close()



def phase_wprep(c):
    nc = c.nc
    t = c.t
    glist = []
    for nm, nmc in (("Wk", KC), ("Wv", KC), ("Wq", FC), ("Wo", FC)):
        for mc in range(nmc):
            glist.append((nm, mc))
    G = len(glist)

    with contextlib.ExitStack() as ps:
        nat = [c.sbuf(ps, f"wp_nat{i}", [128, D], F32) for i in range(2)]
        hib = [c.sbuf(ps, f"wp_hi{i}", [128, D], BF16) for i in range(2)]
        lob = [c.sbuf(ps, f"wp_lo{i}", [128, D], BF16) for i in range(2)]
        asm_h = [c.sbuf(ps, f"wp_ah{i}", [128, D], BF16) for i in range(2)]
        asm_l = [c.sbuf(ps, f"wp_al{i}", [128, D], BF16) for i in range(2)]
        ptr = [c.psum(ps, f"wp_ptr{i}", [128, 1024], BF16) for i in range(4)]

        gp0, dv0, ac0, pe0 = c.gp.n, c.dv.n, c.ac.n, c.pe.n
        qn = [c.q[3], c.q[4]]
        qn_base = [q.n for q in qn]
        qsh = [c.q[5], c.q[6]]
        qsl = [c.q[7], c.q[8]]
        qsh_base = [q.n for q in qsh]
        qsl_base = [q.n for q in qsl]

        with nc.Block() as blk:
            def b_sync(E):
                for g, (nm, mc) in enumerate(glist):
                    if g >= 2:
                        E.wait(c.dv, dv0 + 5 * (g - 2) + 1)
                    E.dma(qn[g % 2], nat[g % 2][:],
                          c.d[nm][mc * 128:(mc + 1) * 128, :])
            c.sync_body(blk, b_sync)

            def b_vector(E):
                for g in range(G):
                    E.wait(c.ac, ac0 + 5 * g + 1)
                    if g >= 2:
                        E.wait(c.pe, pe0 + 8 * (g - 1))
                    E.tensor_tensor(out=lob[g % 2][:], in0=nat[g % 2][:],
                                    in1=hib[g % 2][:], op=ALU.subtract)
                    for bb in range(4, 8):
                        gb = g * 8 + bb
                        E.wait(c.pe, pe0 + gb + 1)
                        if bb == 4 and g >= 2:
                            E.wait(qsl[g % 2], qsl_base[g % 2] + 16 * (g // 2))
                        E.tensor_copy(
                            out=asm_l[g % 2][:, (bb % 4) * 1024:(bb % 4 + 1) * 1024],
                            in_=ptr[gb % 4][:], dep=False)
            c.dve_body(blk, b_vector)

            def b_tensor(E):
                e = E.e
                for g in range(G):
                    for bb in range(8):
                        src_ = hib[g % 2] if bb < 4 else lob[g % 2]
                        if bb == 0:
                            E.wait(c.ac, ac0 + 5 * g + 1)
                        if bb == 4:
                            E.wait(c.dv, dv0 + 5 * g + 1)
                        gb = g * 8 + bb
                        if gb >= 4:
                            pb = gb - 4
                            pg, pbb = divmod(pb, 8)
                            if pbb < 4:
                                E.wait(c.ac, ac0 + 5 * pg + 1 + pbb + 1)
                            else:
                                E.wait(c.dv, dv0 + 5 * pg + 1 + (pbb - 4) + 1)
                        for tt in range(8):
                            dc = (bb % 4) * 8 + tt
                            ins = e.transpose(
                                out=ptr[gb % 4][:, tt * 128:(tt + 1) * 128],
                                in_=src_[:, dc * 128:(dc + 1) * 128],
                                identity=t["id_bf"][:])
                        c.pe.inc(ins)
            c.pe_body(blk, b_tensor)

            def b_act(E):
                for g, (nm, mc) in enumerate(glist):
                    E.wait(qn[g % 2], qn_base[g % 2] + 16 * (g // 2 + 1))
                    if g >= 2:
                        E.wait(c.pe, pe0 + 8 * (g - 1))
                    E.activation(out=hib[g % 2][:], in_=nat[g % 2][:],
                                 func=AF.Copy)
                    for bb in range(4):
                        gb = g * 8 + bb
                        E.wait(c.pe, pe0 + gb + 1)
                        if bb == 0 and g >= 2:
                            E.wait(qsh[g % 2], qsh_base[g % 2] + 16 * (g // 2))
                        E.activation(
                            out=asm_h[g % 2][:, bb * 1024:(bb + 1) * 1024],
                            in_=ptr[gb % 4][:], func=AF.Copy, dep=False)
                    E.wait(c.ac, c.ac.n)
                    E.dma(qsh[g % 2], c.d["w" + nm[1:].lower() + "T_hi"][mc],
                          asm_h[g % 2][:])
                    E.wait(c.dv, dv0 + 5 * g + 5)
                    E.dma(qsl[g % 2], c.d["w" + nm[1:].lower() + "T_lo"][mc],
                          asm_l[g % 2][:])
            c.act_body(blk, b_act)

        assert c.pe.n == pe0 + 8 * G
        assert c.ac.n == ac0 + 5 * G
        assert c.dv.n == dv0 + 5 * G



def phase_quarters(c):
    nc = c.nc
    t = c.t
    kvscope = contextlib.ExitStack()
    ps_kv = c.psum(kvscope, "ps_kv", [128, 1024], F32)
    c._kv_scope = kvscope
    c._ps_kv = ps_kv

    ps = contextlib.ExitStack()
    x0t = [c.sbuf(ps, f"x0t{i}", [128, D], F32) for i in range(2)]
    un_t = c.sbuf(ps, "un_t", [128, D], F32)
    ln1h_t = c.sbuf(ps, "ln1h_t", [128, D], BF16)
    ln1l_t = c.sbuf(ps, "ln1l_t", [128, D], BF16)
    stat = c.sbuf(ps, "stat", [128, 56], F32)
    lnfm_h = c.sbuf(ps, "lnfm_h", [128, FC * 512], BF16)
    lnfm_l = c.sbuf(ps, "lnfm_l", [128, FC * 512], BF16)
    wtb_h = [c.sbuf(ps, f"wtb_h{i}", [128, D], BF16) for i in range(2)]
    wtb_l = [c.sbuf(ps, f"wtb_l{i}", [128, D], BF16) for i in range(2)]
    kfm_h = c.sbuf(ps, "kfm_h", [128, KC * 512], BF16)
    kfm_l = c.sbuf(ps, "kfm_l", [128, KC * 512], BF16)
    vfm_h = c.sbuf(ps, "vfm_h", [128, KC * 512], BF16)
    vfm_l = c.sbuf(ps, "vfm_l", [128, KC * 512], BF16)
    eb = [c.sbuf(ps, f"eb{i}", [128, 512], BF16) for i in range(2)]
    ps_tr = [c.psum(ps, f"qt_tr{i}", [128, 1024], BF16) for i in range(2)]
    ps_proj = [c.psum(ps, f"qt_proj{i}", [128, 512], F32) for i in range(2)]

    lnf_t = un_t
    proj_f32 = un_t[:, 0:512]
    phi_e = un_t[:, 512:1024]
    phi_r = un_t[:, 1024:1536]

    dv0, ac0, gp0, pe0 = c.dv.n, c.ac.n, c.gp.n, c.pe.n
    DVQ, ACQ, GPQ, PEQ = 176, 44, 16, 112

    def dvq(q):
        return dv0 + q * DVQ

    def acq(q):
        return ac0 + q * ACQ

    def gpq(q):
        return gp0 + 1 + q * GPQ

    def peq(q):
        return pe0 + q * PEQ

    def ptr_release_dv(U):
        q, u = divmod(U, 64)
        if u < 32:
            cc, bb = divmod(u, 8)
            return dvq(q) + 26 * cc + 18 + bb + 1
        e = u - 32
        return dvq(q) + 144 + e + 1

    qx = [c.q[9], c.q[10]]
    qx_base = [q.n for q in qx]
    q15_base = c.q[15].n
    qwh = [c.q[11], c.q[12]]
    qwl = [c.q[13], c.q[14]]
    qwh_base = [q.n for q in qwh]
    qwl_base = [q.n for q in qwl]

    def lnfm_view(buf, bb, cc):
        v = buf[:].rearrange("p (dc t) -> p dc t", dc=FC)
        return v[:, bb * 8:(bb + 1) * 8, cc * 128:(cc + 1) * 128]

    with nc.Block() as blk:
        def b_sync(E):
            for q in range(4):
                for cc in range(4):
                    ti = q * 4 + cc
                    if ti >= 2:
                        qp, cp = divmod(ti - 2, 4)
                        E.wait(c.ac, acq(qp) + 3 * cp + 3)
                        E.wait(c.dv, dvq(qp) + 26 * cp + 17)
                    r0 = q * 512 + cc * 128
                    E.dma(qx[ti % 2], x0t[ti % 2][:], c.d["x"][r0:r0 + 128, :])
                for mc in range(16):
                    wi = q * 16 + mc
                    if wi >= 2:
                        qp, mp = divmod(wi - 2, 16)
                        E.wait(c.pe, peq(qp) + 32 + mp + 1)
                    if mc < 8:
                        hi_src = c.d["wkT_hi"][mc]
                        lo_src = c.d["wkT_lo"][mc]
                    else:
                        hi_src = c.d["wvT_hi"][mc - 8]
                        lo_src = c.d["wvT_lo"][mc - 8]
                    E.dma(qwh[mc % 2], wtb_h[mc % 2][:], hi_src)
                    E.dma(qwl[mc % 2], wtb_l[mc % 2][:], lo_src)
                if q < 2:
                    E.wait(c.dv, dvq(q) + 104)
                    E.dma(c.q[15], c.d["ln1fm_hi"][q], lnfm_h[:])
                    E.dma(c.q[15], c.d["ln1fm_lo"][q], lnfm_l[:])
        c.sync_body(blk, b_sync)

        def b_pool(E):
            E.memset(t["scol"][:], 0.0)
            for q in range(4):
                for mc in range(8):
                    if q >= 1 and mc == 0:
                        E.wait(c.pe, peq(q - 1) + 112)
                    E.wait(c.dv, dvq(q) + 104 + 4 * mc + 1)
                    E.tensor_copy(out=kfm_h[:, mc * 512:(mc + 1) * 512],
                                  in_=proj_f32)
                for j in range(8):
                    E.wait(c.ac, acq(q) + 36 + j + 1)
                    E.tensor_copy(out=vfm_h[:, j * 512:(j + 1) * 512],
                                  in_=proj_f32)
        c.pool_body(blk, b_pool)

        def b_act(E):
            for q in range(4):
                for cc in range(4):
                    ti = q * 4 + cc
                    E.wait(c.dv, dvq(q) + 26 * cc + 10)
                    E.activation(out=stat[:, 51:52], in_=stat[:, 50:51],
                                 func=AF.Ln)
                    E.activation(out=stat[:, 51:52], in_=stat[:, 51:52],
                                 func=AF.Exp, scale=-0.5)
                    E.wait(qx[ti % 2], qx_base[ti % 2] + 16 * (ti // 2 + 1))
                    E.wait(c.dv, dvq(q) + 26 * cc + 16)
                    if ti >= 1:
                        qp2, cp2 = divmod(ti - 1, 4)
                        E.wait(c.dv, dvq(qp2) + 26 * cp2 + 18)
                        E.wait(c.pe, peq(qp2) + 8 * cp2 + 4)
                    E.activation(out=ln1h_t[:], in_=x0t[ti % 2][:],
                                 func=AF.Identity,
                                 scale=stat[:, 53:54], bias=stat[:, 54:55])
                for mc in range(8):
                    E.wait(c.pe, peq(q) + 32 + mc + 1)
                    if mc > 0:
                        E.wait(c.dv, dvq(q) + 104 + 4 * (mc - 1) + 1)
                    elif q > 0:
                        E.wait(c.dv, dvq(q - 1) + 104 + 4 * 7 + 1)
                    E.activation(out=phi_e, in_=ps_proj[mc % 2][:],
                                 func=AF.Relu, scale=-1.0,
                                 bias=t["nbias_cols"][:, 32 + mc:33 + mc])
                    E.activation(out=phi_e, in_=phi_e, func=AF.Exp, scale=-1.0)
                    E.activation(out=phi_r, in_=ps_proj[mc % 2][:],
                                 func=AF.Relu,
                                 bias=t["bias_cols"][:, 32 + mc:33 + mc])
                for j in range(8):
                    mc = 8 + j
                    E.wait(c.pe, peq(q) + 32 + mc + 1)
                    E.wait(c.gp, gpq(q) + 8 + j)
                    E.wait(c.dv, dvq(q) + 136 + j)
                    E.activation(out=proj_f32, in_=ps_proj[mc % 2][:],
                                 func=AF.Identity,
                                 bias=t["bias_cols"][:, 40 + j:41 + j])
        c.act_body(blk, b_act)

        def b_dve(E):
            for q in range(4):
                for cc in range(4):
                    ti = q * 4 + cc
                    E.wait(qx[ti % 2], qx_base[ti % 2] + 16 * (ti // 2 + 1))
                    if cc == 0 and q > 0:
                        E.wait(c.gp, gpq(q - 1) + 16)
                        if q <= 2:
                            E.wait(c.q[15], q15_base + 32 * q)
                    for sb8 in range(8):
                        E.bn_stats(out=stat[:, 6 * sb8:6 * sb8 + 6],
                                   in_=x0t[ti % 2][:, sb8 * 512:(sb8 + 1) * 512],
                                   dep=(sb8 == 0))
                    E.bn_aggr(out=stat[:, 48:50], in_=stat[:, 0:48])
                    E.tensor_scalar(out=stat[:, 50:51], in0=stat[:, 49:50],
                                    scalar1=1.0, scalar2=EPS_LN,
                                    op0=ALU.mult, op1=ALU.add)
                    E.wait(c.ac, acq(q) + 3 * cc + 2)
                    E.tensor_tensor(out=stat[:, 52:53], in0=stat[:, 51:52],
                                    in1=stat[:, 51:52], op=ALU.mult)
                    E.tensor_tensor(out=stat[:, 52:53], in0=stat[:, 52:53],
                                    in1=stat[:, 50:51], op=ALU.mult)
                    E.tensor_scalar(out=stat[:, 52:53], in0=stat[:, 52:53],
                                    scalar1=-0.5, scalar2=1.5,
                                    op0=ALU.mult, op1=ALU.add)
                    E.tensor_tensor(out=stat[:, 53:54], in0=stat[:, 51:52],
                                    in1=stat[:, 52:53], op=ALU.mult)
                    E.tensor_tensor(out=stat[:, 54:55], in0=stat[:, 48:49],
                                    in1=stat[:, 53:54], op=ALU.mult)
                    E.tensor_scalar_mul(out=stat[:, 54:55], in0=stat[:, 54:55],
                                        scalar1=-1.0)
                    E.tensor_scalar(out=lnf_t[:], in0=x0t[ti % 2][:],
                                    scalar1=stat[:, 53:54],
                                    scalar2=stat[:, 54:55],
                                    op0=ALU.mult, op1=ALU.add)
                    E.wait(c.ac, acq(q) + 3 * cc + 3)
                    if ti >= 1:
                        qp, cp = divmod(ti - 1, 4)
                        E.wait(c.pe, peq(qp) + 8 * cp + 8)
                    E.tensor_tensor(out=ln1l_t[:], in0=lnf_t[:], in1=ln1h_t[:],
                                    op=ALU.subtract)
                    for bb in range(8):
                        E.wait(c.pe, peq(q) + 8 * cc + bb + 1)
                        dsth = lnfm_h if bb < 4 else lnfm_l
                        E.tensor_copy(out=lnfm_view(dsth, bb % 4, cc),
                                      in_=ps_tr[(64 * q + 8 * cc + bb) % 2][:],
                                      dep=False)
                for mc in range(8):
                    E.wait(c.ac, acq(q) + 12 + 3 * mc + 3)
                    if mc > 0:
                        E.wait(c.gp, gpq(q) + mc)
                    elif q > 0:
                        E.wait(c.gp, gpq(q - 1) + 16)
                    else:
                        E.wait(c.gp, gp0 + 1)
                    E.tensor_tensor(out=proj_f32, in0=phi_e, in1=phi_r,
                                    op=ALU.add)
                    E.reduce_sum(out=stat[:, 55:56], in_=proj_f32, axis=AX.X)
                    E.tensor_tensor(out=t["scol"][:, mc:mc + 1],
                                    in0=t["scol"][:, mc:mc + 1],
                                    in1=stat[:, 55:56], op=ALU.add)
                    E.wait(c.gp, gpq(q) + mc + 1)
                    E.tensor_tensor(out=kfm_l[:, mc * 512:(mc + 1) * 512],
                                    in0=proj_f32,
                                    in1=kfm_h[:, mc * 512:(mc + 1) * 512],
                                    op=ALU.subtract)
                for j in range(8):
                    E.wait(c.gp, gpq(q) + 8 + j + 1)
                    E.tensor_tensor(out=vfm_l[:, j * 512:(j + 1) * 512],
                                    in0=proj_f32,
                                    in1=vfm_h[:, j * 512:(j + 1) * 512],
                                    op=ALU.subtract)
                for e in range(32):
                    U = q * 64 + 32 + e
                    E.wait(c.pe, peq(q) + 48 + 2 * e + 1)
                    if e >= 2:
                        E.wait(c.pe, peq(q) + 48 + 2 * (e - 2) + 2)
                    elif q > 0:
                        E.wait(c.pe, peq(q - 1) + 48 + 2 * (30 + e) + 2)
                    E.tensor_copy(out=eb[e % 2][:], in_=ps_tr[U % 2][:, 0:512],
                                  dep=False)
        c.dve_body(blk, b_dve)

        def b_pe(E):
            e = E.e
            kv_started = set()
            for q in range(4):
                for cc in range(4):
                    for bb in range(8):
                        if bb == 0:
                            E.wait(c.ac, acq(q) + 3 * cc + 3)
                        if bb == 4:
                            E.wait(c.dv, dvq(q) + 26 * cc + 18)
                        U = q * 64 + 8 * cc + bb
                        if U >= 2:
                            E.wait(c.dv, ptr_release_dv(U - 2))
                        src = ln1h_t if bb < 4 else ln1l_t
                        for tt2 in range(8):
                            dc = (bb % 4) * 8 + tt2
                            ins = e.transpose(
                                out=ps_tr[U % 2][:, tt2 * 128:(tt2 + 1) * 128],
                                in_=src[:, dc * 128:(dc + 1) * 128],
                                identity=t["id_bf"][:])
                        c.pe.inc(ins)
                E.wait(c.dv, dvq(q) + 104)
                for mc in range(16):
                    wi = q * 16 + mc
                    par = mc % 2
                    E.wait(qwh[par], qwh_base[par] + 16 * (wi // 2 + 1))
                    E.wait(qwl[par], qwl_base[par] + 16 * (wi // 2 + 1))
                    if wi >= 2:
                        qp, mp = divmod(wi - 2, 16)
                        if mp < 8:
                            E.wait(c.ac, acq(qp) + 12 + 3 * mp + 3)
                        else:
                            E.wait(c.ac, acq(qp) + 36 + (mp - 8) + 1)
                    for dc in range(FC):
                        st = (dc == 0)
                        sp = (dc == FC - 1)
                        e.matmul(out=ps_proj[par][:],
                                 lhsT=wtb_h[par][:, dc * 128:(dc + 1) * 128],
                                 rhs=lnfm_h[:, dc * 512:(dc + 1) * 512],
                                 start=st, stop=False)
                        e.matmul(out=ps_proj[par][:],
                                 lhsT=wtb_h[par][:, dc * 128:(dc + 1) * 128],
                                 rhs=lnfm_l[:, dc * 512:(dc + 1) * 512],
                                 start=False, stop=False)
                        ins = e.matmul(out=ps_proj[par][:],
                                       lhsT=wtb_l[par][:, dc * 128:(dc + 1) * 128],
                                       rhs=lnfm_h[:, dc * 512:(dc + 1) * 512],
                                       start=False, stop=sp)
                    c.pe.inc(ins)
                E.wait(c.dv, dvq(q) + 144)
                E.wait(c.gp, gpq(q) + 16)
                for e2 in range(32):
                    kv, cc = divmod(e2, 4)
                    U = q * 64 + 32 + e2
                    E.wait(c.dv, ptr_release_dv(U - 2))
                    for buf, slot in ((kfm_h, 0), (kfm_l, 1), (vfm_h, 2),
                                      (vfm_l, 3)):
                        ins = e.transpose(
                            out=ps_tr[U % 2][:, slot * 128:(slot + 1) * 128],
                            in_=buf[:, kv * 512 + cc * 128:kv * 512 + (cc + 1) * 128],
                            identity=t["id_bf"][:])
                    c.pe.inc(ins)
                    E.wait(c.dv, dvq(q) + 144 + e2 + 1)
                    bank = kv // 4
                    st = bank not in kv_started
                    kv_started.add(bank)
                    out = ps_kv[:, kv * 128:(kv + 1) * 128]
                    e.matmul(out=out, lhsT=eb[e2 % 2][:, 0:128],
                             rhs=eb[e2 % 2][:, 256:384], start=st, stop=False)
                    e.matmul(out=out, lhsT=eb[e2 % 2][:, 0:128],
                             rhs=eb[e2 % 2][:, 384:512], start=False, stop=False)
                    ins = e.matmul(out=out, lhsT=eb[e2 % 2][:, 128:256],
                                   rhs=eb[e2 % 2][:, 256:384], start=False,
                                   stop=(q == 3 and e2 in (15, 31)))
                    c.pe.inc(ins)
        c.pe_body(blk, b_pe)

    assert c.dv.n == dv0 + 4 * DVQ, (c.dv.n, dv0 + 4 * DVQ)
    assert c.ac.n == acq(4), (c.ac.n, acq(4))
    assert c.gp.n == gp0 + 1 + 4 * GPQ
    assert c.pe.n == pe0 + 4 * PEQ
    c._qt_scope = ps



def phase_attn_a(c, s):
    nc = c.nc
    t = c.t
    with contextlib.ExitStack() as ps:
        lnq_h = c.sbuf(ps, f"lnq_h_s{s}", [128, FC * 512], BF16)
        lnq_l = c.sbuf(ps, f"lnq_l_s{s}", [128, FC * 512], BF16)
        attn_h = c.sbuf(ps, f"attn_h_s{s}", [128, FC * 512], BF16)
        attn_l = c.sbuf(ps, f"attn_l_s{s}", [128, FC * 512], BF16)
        wq_h = [c.sbuf(ps, f"aw_h{i}_s{s}", [128, D], BF16) for i in range(3)]
        wq_l = [c.sbuf(ps, f"aw_l{i}_s{s}", [128, D], BF16) for i in range(3)]
        q_f32 = c.sbuf(ps, f"q_f32_s{s}", [128, 512], F32)
        phi_e2 = c.sbuf(ps, f"phi_e2_s{s}", [128, 512], F32)
        phi_r2 = c.sbuf(ps, f"phi_r2_s{s}", [128, 512], F32)
        at_f32 = c.sbuf(ps, f"at_f32_s{s}", [128, 512], F32)
        phiq_h = [c.sbuf(ps, f"phiq_h{i}_s{s}", [128, 512], BF16) for i in range(2)]
        phiq_l = [c.sbuf(ps, f"phiq_l{i}_s{s}", [128, 512], BF16) for i in range(2)]
        o_f32 = c.sbuf(ps, f"o_f32_s{s}", [128, 512], F32)
        o_tr = [c.sbuf(ps, f"o_tr{i}_s{s}", [128, 512], F32) for i in range(2)]
        ps_q = [c.psum(ps, f"a_psq{i}_s{s}", [128, 512], F32) for i in range(3)]
        ps_at = [c.psum(ps, f"a_psat{i}_s{s}", [128, 512], F32) for i in range(3)]
        ps_trf = c.psum(ps, f"a_pstrf_s{s}", [128, 512], F32)

        dv0, ac0, gp0, pe0 = c.dv.n, c.ac.n, c.gp.n, c.pe.n
        q0 = c.q[0]
        q0b = q0.n
        qwh = [c.q[11], c.q[12], c.q[8]]
        qwl = [c.q[13], c.q[14], c.q[9]]
        qwh_base = [q.n for q in qwh]
        qwl_base = [q.n for q in qwl]
        qo = c.qsw[4 * s:4 * s + 4]
        qo_base = [q.n for q in qo]

        def pe_q_at(h):
            return (pe0 + 1) if h == 0 else (pe0 + 2 * h)

        def pe_at_at(h):
            return (pe0 + 64) if h == 31 else (pe0 + 2 * h + 3)

        with nc.Block() as blk:
            def b_sync(E):
                E.dma(q0, lnq_h[:], c.d["ln1fm_hi"][s])
                E.dma(q0, lnq_l[:], c.d["ln1fm_lo"][s])
                for g in range(64):
                    if g >= 3:
                        if g < 35:
                            E.wait(c.pe, pe_at_at(g - 3))
                        else:
                            E.wait(c.pe, pe0 + 64 + 2 * (g - 32 - 2) + 1)
                    if g < 32:
                        hi_src = c.d["wqT_hi"][g]
                        lo_src = c.d["wqT_lo"][g]
                    else:
                        hi_src = c.d["woT_hi"][g - 32]
                        lo_src = c.d["woT_lo"][g - 32]
                    E.dma(qwh[g % 3], wq_h[g % 3][:], hi_src)
                    E.dma(qwl[g % 3], wq_l[g % 3][:], lo_src)
            c.sync_body(blk, b_sync)

            def b_pe(E):
                e = E.e
                E.wait(q0, q0b + 32)

                def emit_q(h):
                    par = h % 3
                    bpar = h % 3
                    E.wait(qwh[bpar], qwh_base[bpar] + 16 * (h // 3 + 1))
                    E.wait(qwl[bpar], qwl_base[bpar] + 16 * (h // 3 + 1))
                    if h >= 3:
                        E.wait(c.ac, ac0 + 4 * (h - 3) + 3)
                    for dc in range(FC):
                        st = (dc == 0)
                        sp = (dc == FC - 1)
                        e.matmul(out=ps_q[par][:],
                                 lhsT=wq_h[bpar][:, dc * 128:(dc + 1) * 128],
                                 rhs=lnq_h[:, dc * 512:(dc + 1) * 512],
                                 start=st, stop=False)
                        e.matmul(out=ps_q[par][:],
                                 lhsT=wq_h[bpar][:, dc * 128:(dc + 1) * 128],
                                 rhs=lnq_l[:, dc * 512:(dc + 1) * 512],
                                 start=False, stop=False)
                        ins = e.matmul(out=ps_q[par][:],
                                       lhsT=wq_l[bpar][:, dc * 128:(dc + 1) * 128],
                                       rhs=lnq_h[:, dc * 512:(dc + 1) * 512],
                                       start=False, stop=sp)
                    c.pe.inc(ins)

                def emit_at(h):
                    par = h % 3
                    bpar = h % 2
                    kv = h // 4
                    E.wait(c.dv, dv0 + 3 * h + 2)
                    E.wait(c.gp, gp0 + 2 * h + 1)
                    if h >= 3:
                        E.wait(c.ac, ac0 + 4 * (h - 3) + 4)
                    kvs = slice(kv * 128, (kv + 1) * 128)
                    e.matmul(out=ps_at[par][:], lhsT=t["kvt_hi"][:, kvs],
                             rhs=phiq_h[bpar][:], start=True, stop=False)
                    e.matmul(out=ps_at[par][:], lhsT=t["kvt_hi"][:, kvs],
                             rhs=phiq_l[bpar][:], start=False, stop=False)
                    ins = e.matmul(out=ps_at[par][:], lhsT=t["kvt_lo"][:, kvs],
                                   rhs=phiq_h[bpar][:], start=False, stop=True)
                    c.pe.inc(ins)

                emit_q(0)
                for h in range(1, 32):
                    emit_q(h)
                    emit_at(h - 1)
                emit_at(31)
                E.wait(c.dv, dv0 + 96)
                E.wait(c.gp, gp0 + 64)
                for mc in range(FC):
                    g = 32 + mc
                    par = (32 + mc) % 3
                    bpar = g % 3
                    E.wait(qwh[bpar], qwh_base[bpar] + 16 * (g // 3 + 1))
                    E.wait(qwl[bpar], qwl_base[bpar] + 16 * (g // 3 + 1))
                    if mc >= 3:
                        E.wait(c.ac, ac0 + 128 + (mc - 3) + 1)
                    else:
                        E.wait(c.ac, ac0 + 4 * (29 + mc) + 3)
                    for fc in range(FC):
                        st = (fc == 0)
                        sp = (fc == FC - 1)
                        e.matmul(out=ps_q[par][:],
                                 lhsT=wq_h[bpar][:, fc * 128:(fc + 1) * 128],
                                 rhs=attn_h[:, fc * 512:(fc + 1) * 512],
                                 start=st, stop=False)
                        e.matmul(out=ps_q[par][:],
                                 lhsT=wq_l[bpar][:, fc * 128:(fc + 1) * 128],
                                 rhs=attn_h[:, fc * 512:(fc + 1) * 512],
                                 start=False, stop=False)
                        ins = e.matmul(out=ps_q[par][:],
                                       lhsT=wq_h[bpar][:, fc * 128:(fc + 1) * 128],
                                       rhs=attn_l[:, fc * 512:(fc + 1) * 512],
                                       start=False, stop=sp)
                    c.pe.inc(ins)
                    E.wait(c.ac, ac0 + 128 + mc + 1)
                    if mc >= 1:
                        E.wait(c.dv, dv0 + 96 + (mc - 1) + 1)
                    for tt2 in range(4):
                        ins = e.transpose(
                            out=ps_trf[:, tt2 * 128:(tt2 + 1) * 128],
                            in_=o_f32[:, tt2 * 128:(tt2 + 1) * 128],
                            identity=t["id_f32"][:])
                    c.pe.inc(ins)
            c.pe_body(blk, b_pe)

            def b_act(E):
                for h in range(32):
                    par = h % 3
                    E.wait(c.pe, pe_q_at(h))
                    if h >= 1:
                        E.wait(c.dv, dv0 + 3 * (h - 1) + 1)
                    E.activation(out=phi_e2[:], in_=ps_q[par][:], func=AF.Relu,
                                 scale=-1.0, bias=t["nbias_cols"][:, h:h + 1])
                    E.activation(out=phi_e2[:], in_=phi_e2[:], func=AF.Exp,
                                 scale=-1.0)
                    E.activation(out=phi_r2[:], in_=ps_q[par][:], func=AF.Relu,
                                 bias=t["bias_cols"][:, h:h + 1])
                    E.wait(c.pe, pe_at_at(h))
                    if h >= 1:
                        E.wait(c.gp, gp0 + 2 * (h - 1) + 2)
                        E.wait(c.dv, dv0 + 3 * (h - 1) + 3)
                    kv = h // 4
                    E.activation(out=at_f32[:], in_=ps_at[par][:],
                                 func=AF.Identity,
                                 scale=t["recip"][:, kv:kv + 1])
                for mc in range(FC):
                    E.wait(c.pe, pe0 + 64 + 2 * mc + 1)
                    E.activation(out=o_f32[:], in_=ps_q[(32 + mc) % 3][:],
                                 func=AF.Identity,
                                 bias=t["bias_cols"][:, 48 + mc:49 + mc])
            c.act_body(blk, b_act)

            def b_dve(E):
                for h in range(32):
                    E.wait(c.ac, ac0 + 4 * h + 3)
                    if h >= 1:
                        E.wait(c.gp, gp0 + 2 * (h - 1) + 1)
                    E.tensor_tensor(out=q_f32[:], in0=phi_e2[:], in1=phi_r2[:],
                                    op=ALU.add)
                    E.wait(c.gp, gp0 + 2 * h + 1)
                    if h >= 2:
                        E.wait(c.pe, pe_at_at(h - 2))
                    E.tensor_tensor(out=phiq_l[h % 2][:], in0=q_f32[:],
                                    in1=phiq_h[h % 2][:], op=ALU.subtract)
                    E.wait(c.gp, gp0 + 2 * h + 2)
                    E.wait(c.ac, ac0 + 4 * h + 4)
                    E.tensor_tensor(out=attn_l[:, h * 512:(h + 1) * 512],
                                    in0=at_f32[:],
                                    in1=attn_h[:, h * 512:(h + 1) * 512],
                                    op=ALU.subtract)
                for mc in range(FC):
                    E.wait(c.pe, pe0 + 64 + 2 * mc + 2)
                    if mc >= 2:
                        for i in range(4):
                            E.wait(qo[i], qo_base[i] + 16 * (mc - 1))
                    E.tensor_copy(out=o_tr[mc % 2][:], in_=ps_trf[:])
            c.dve_body(blk, b_dve)

            def b_pool(E):
                for h in range(32):
                    E.wait(c.dv, dv0 + 3 * h + 1)
                    if h >= 2:
                        E.wait(c.pe, pe_at_at(h - 2))
                    E.tensor_copy(out=phiq_h[h % 2][:], in_=q_f32[:])
                    E.wait(c.ac, ac0 + 4 * h + 4)
                    E.tensor_copy(out=attn_h[:, h * 512:(h + 1) * 512],
                                  in_=at_f32[:])
                tg = s * 4
                for mc in range(FC):
                    E.wait(c.dv, dv0 + 96 + mc + 1)
                    for tt2 in range(4):
                        E.dma(qo[tt2], c.d["o_tm"][tg + tt2, mc],
                              o_tr[mc % 2][:, tt2 * 128:(tt2 + 1) * 128])
            c.pool_body(blk, b_pool)

            if s == 0 and ("dbg_attnh" in DEBUG_TAPS):
                d_ah = c.dram("dbg_attnh", [128, FC * 512], BF16)
                d_al = c.dram("dbg_attnl", [128, FC * 512], BF16)
                d_lq = c.dram("dbg_lnqh", [128, FC * 512], BF16)

                def b_dbg(E):
                    E.wait(c.dv, dv0 + 96)
                    E.wait(c.gp, gp0 + 64)
                    E.dma(c.q[15], d_ah, attn_h[:])
                    E.dma(c.q[15], d_al, attn_l[:])
                    E.dma(c.q[15], d_lq, lnq_h[:])
                c.scalarq_body(blk, b_dbg)

        assert c.pe.n == pe0 + 128
        assert c.ac.n == ac0 + 160
        assert c.dv.n == dv0 + 128
        assert c.gp.n == gp0 + 64



def _ln_recipe_dve(E, c, x_ap, stat2, ac_y0):
    for sb8 in range(8):
        E.bn_stats(out=stat2[:, 6 * sb8:6 * sb8 + 6],
                   in_=x_ap[:, sb8 * 512:(sb8 + 1) * 512], dep=(sb8 == 0))
    E.bn_aggr(out=stat2[:, 48:50], in_=stat2[:, 0:48])
    E.tensor_scalar(out=stat2[:, 50:51], in0=stat2[:, 49:50], scalar1=1.0,
                    scalar2=EPS_LN, op0=ALU.mult, op1=ALU.add)
    E.wait(c.ac, ac_y0)
    E.tensor_tensor(out=stat2[:, 52:53], in0=stat2[:, 51:52],
                    in1=stat2[:, 51:52], op=ALU.mult)
    E.tensor_tensor(out=stat2[:, 52:53], in0=stat2[:, 52:53],
                    in1=stat2[:, 50:51], op=ALU.mult)
    E.tensor_scalar(out=stat2[:, 52:53], in0=stat2[:, 52:53], scalar1=-0.5,
                    scalar2=1.5, op0=ALU.mult, op1=ALU.add)
    E.tensor_tensor(out=stat2[:, 53:54], in0=stat2[:, 51:52],
                    in1=stat2[:, 52:53], op=ALU.mult)
    E.tensor_tensor(out=stat2[:, 54:55], in0=stat2[:, 48:49],
                    in1=stat2[:, 53:54], op=ALU.mult)
    E.tensor_scalar_mul(out=stat2[:, 54:55], in0=stat2[:, 54:55], scalar1=-1.0)


def phase_attn_b(c, s):
    nc = c.nc
    t = c.t
    with contextlib.ExitStack() as ps:
        o_t = [c.sbuf(ps, f"bo_t{i}_s{s}", [128, D], F32) for i in range(2)]
        x0b = [c.sbuf(ps, f"bx0{i}_s{s}", [128, D], F32) for i in range(2)]
        x1_t = c.sbuf(ps, f"bx1_s{s}", [128, D], F32)
        lnf2 = c.sbuf(ps, f"blnf_s{s}", [128, D], F32)
        l3h_t = c.sbuf(ps, f"bl3h_s{s}", [128, D], BF16)
        l3l_t = c.sbuf(ps, f"bl3l_s{s}", [128, D], BF16)
        asm3_h = c.sbuf(ps, f"basm_h_s{s}", [128, D], BF16)
        asm3_l = c.sbuf(ps, f"basm_l_s{s}", [128, D], BF16)
        stat2 = c.sbuf(ps, f"bstat_s{s}", [128, 56], F32)
        ps_tr3 = [c.psum(ps, f"b_tr{i}_s{s}", [128, 1024], BF16) for i in range(2)]

        dv0, ac0, pe0 = c.dv.n, c.ac.n, c.pe.n
        DVT, ACT_T, PET = 44, 5, 8
        qo_ld = [c.q[4], c.q[5]]
        qx_ld = [c.q[6], c.q[7]]
        qst = [c.q[8], c.q[9], c.q[10]]
        qo_b = [q.n for q in qo_ld]
        qx_b = [q.n for q in qx_ld]
        qst_b = [q.n for q in qst]

        def dvt(cc):
            return dv0 + cc * DVT

        def act(cc):
            return ac0 + cc * ACT_T

        def pet(cc):
            return pe0 + cc * PET

        with nc.Block() as blk:
            def b_sync(E):
                for cc in range(4):
                    tg = s * 4 + cc
                    if cc >= 2:
                        E.wait(c.dv, dvt(cc - 2) + 18)
                    E.dma(qo_ld[cc % 2],
                          o_t[cc % 2][:].rearrange("p (m t) -> p m t", m=FC),
                          c.d["o_tm"][tg].rearrange("m p t -> p m t"))
                    E.dma(qx_ld[cc % 2], x0b[cc % 2][:],
                          c.d["x"][tg * 128:(tg + 1) * 128, :])
                    E.wait(c.dv, dvt(cc) + 18)
                    E.dma(qst[0], c.d["x1"][tg * 128:(tg + 1) * 128, :], x1_t[:])
                    E.wait(c.dv, dvt(cc) + 44)
                    E.dma(qst[1],
                          c.d["ln3fm_hi"][s, :, cc],
                          asm3_h[:].rearrange("p (dc t) -> p dc t", dc=FC))
                    E.dma(qst[2],
                          c.d["ln3fm_lo"][s, :, cc],
                          asm3_l[:].rearrange("p (dc t) -> p dc t", dc=FC))
            c.sync_body(blk, b_sync)

            def b_dve(E):
                for cc in range(4):
                    E.wait(qo_ld[cc % 2], qo_b[cc % 2] + 16 * (cc // 2 + 1))
                    if cc >= 1:
                        E.wait(qst[0], qst_b[0] + 16 * cc)
                    _ln_recipe_dve(E, c, o_t[cc % 2][:], stat2, act(cc) + 2)
                    E.tensor_scalar(out=lnf2[:], in0=o_t[cc % 2][:],
                                    scalar1=stat2[:, 53:54],
                                    scalar2=stat2[:, 54:55],
                                    op0=ALU.mult, op1=ALU.add)
                    E.wait(qx_ld[cc % 2], qx_b[cc % 2] + 16 * (cc // 2 + 1))
                    E.tensor_tensor(out=x1_t[:], in0=lnf2[:],
                                    in1=x0b[cc % 2][:], op=ALU.add)
                    _ln_recipe_dve(E, c, x1_t[:], stat2, act(cc) + 4)
                    E.tensor_scalar(out=lnf2[:], in0=x1_t[:],
                                    scalar1=stat2[:, 53:54],
                                    scalar2=stat2[:, 54:55],
                                    op0=ALU.mult, op1=ALU.add)
                    E.wait(c.ac, act(cc) + 5)
                    E.tensor_tensor(out=l3l_t[:], in0=lnf2[:], in1=l3h_t[:],
                                    op=ALU.subtract)
                    for bb in range(8):
                        E.wait(c.pe, pet(cc) + bb + 1)
                        if cc >= 1 and bb <= 1:
                            E.wait(qst[1], qst_b[1] + 16 * cc)
                            E.wait(qst[2], qst_b[2] + 16 * cc)
                        dst = asm3_h if bb < 4 else asm3_l
                        E.tensor_copy(
                            out=dst[:].rearrange("p (g x) -> p g x", g=4)
                            [:, bb % 4, :],
                            in_=ps_tr3[bb % 2][:], dep=False)
            c.dve_body(blk, b_dve)

            def b_act(E):
                for cc in range(4):
                    E.wait(c.dv, dvt(cc) + 10)
                    E.activation(out=stat2[:, 51:52], in_=stat2[:, 50:51],
                                 func=AF.Ln)
                    E.activation(out=stat2[:, 51:52], in_=stat2[:, 51:52],
                                 func=AF.Exp, scale=-0.5)
                    E.wait(c.dv, dvt(cc) + 28)
                    E.activation(out=stat2[:, 51:52], in_=stat2[:, 50:51],
                                 func=AF.Ln)
                    E.activation(out=stat2[:, 51:52], in_=stat2[:, 51:52],
                                 func=AF.Exp, scale=-0.5)
                    E.wait(c.dv, dvt(cc) + 34)
                    if cc >= 1:
                        E.wait(c.dv, dvt(cc - 1) + 36)
                        E.wait(c.pe, pet(cc - 1) + 4)
                    E.activation(out=l3h_t[:], in_=x1_t[:], func=AF.Identity,
                                 scale=stat2[:, 53:54], bias=stat2[:, 54:55])
            c.act_body(blk, b_act)

            def b_pe(E):
                e = E.e
                for cc in range(4):
                    for bb in range(8):
                        if bb == 0:
                            E.wait(c.ac, act(cc) + 5)
                        if bb == 4:
                            E.wait(c.dv, dvt(cc) + 36)
                        gb = cc * 8 + bb
                        if gb >= 2:
                            E.wait(c.dv, dvt((gb - 2) // 8) + 36 +
                                   ((gb - 2) % 8) + 1)
                        src = l3h_t if bb < 4 else l3l_t
                        for tt2 in range(8):
                            dc = (bb % 4) * 8 + tt2
                            ins = e.transpose(
                                out=ps_tr3[gb % 2][:, tt2 * 128:(tt2 + 1) * 128],
                                in_=src[:, dc * 128:(dc + 1) * 128],
                                identity=t["id_bf"][:])
                        c.pe.inc(ins)
            c.pe_body(blk, b_pe)

        assert c.dv.n == dv0 + 4 * DVT
        assert c.ac.n == ac0 + 4 * ACT_T
        assert c.pe.n == pe0 + 4 * PET



def phase_kvfin(c):
    nc = c.nc
    t = c.t
    ps_kv = c._ps_kv
    c._qt_scope.close()
    dv0, gp0 = c.dv.n, c.gp.n
    with contextlib.ExitStack() as ps:
        kvt_f32 = c.sbuf(ps, "kvt_f32", [128, DKV], F32)
        sc2 = c.sbuf(ps, "sc2", [128, NKV], F32)
        sc3 = c.sbuf(ps, "sc3", [128, NKV], F32)
        with nc.Block() as blk:
            def b_dve(E):
                E.tensor_copy(out=kvt_f32[:], in_=ps_kv[:])
                E.tensor_scalar_add(out=sc2[:], in0=t["scol"][:], scalar1=EPS_AT)
                E.reciprocal(out=t["recip"][:], in_=sc2[:])
                E.tensor_tensor(out=sc3[:], in0=sc2[:], in1=t["recip"][:],
                                op=ALU.mult)
                E.tensor_scalar(out=sc3[:], in0=sc3[:], scalar1=-1.0,
                                scalar2=2.0, op0=ALU.mult, op1=ALU.add)
                E.tensor_tensor(out=t["recip"][:], in0=t["recip"][:],
                                in1=sc3[:], op=ALU.mult)
                E.wait(c.gp, gp0 + 1)
                E.tensor_tensor(out=t["kvt_lo"][:], in0=kvt_f32[:],
                                in1=t["kvt_hi"][:], op=ALU.subtract)
            c.dve_body(blk, b_dve)

            def b_pool(E):
                E.wait(c.dv, dv0 + 1)
                E.tensor_copy(out=t["kvt_hi"][:], in_=kvt_f32[:])
            c.pool_body(blk, b_pool)

            if "dbg_kvt" in DEBUG_TAPS or "dbg_stats" in DEBUG_TAPS:
                dbg_kvt = c.dram("dbg_kvt", [128, DKV], F32)
                dbg_st = c.dram("dbg_stats", [128, 2 * NKV + 8], F32)

                def b_sync(E):
                    E.wait(c.dv, c.dv.n)
                    E.dma(c.q[15], dbg_kvt, kvt_f32[:])
                    E.dma(c.q[15], dbg_st[:, 0:NKV], t["scol"][:])
                    E.dma(c.q[15], dbg_st[:, NKV:2 * NKV], t["recip"][:])
                    E.dma(c.q[15], dbg_st[:, 2 * NKV:], t["scale_half"][:])
                c.sync_body(blk, b_sync)
    c._kv_scope.close()



def _emit_tprep(c, blk_scope, which, tT):
    nc = c.nc
    t = c.t
    down = (which == "down")
    wsrc = c.d["w_" + which]
    sidx = {"gate": 0, "up": 1, "down": 2}[which]
    with contextlib.ExitStack() as ps:
        if not down:
            wnat = [c.sbuf(ps, f"tp_w{i}_{which}", [128, 10 * D8], F32)
                    for i in range(2)]
            wn86 = [c.sbuf(ps, f"tp_w86{i}_{which}", [128, D8], F32)
                    for i in range(2)]
            tnat = c.sbuf(ps, f"tp_t_{which}", [128, 11 * D8], BF16)
            msk = c.sbuf(ps, f"tp_m_{which}", [128, 11 * D8], BF16)
            sgn = c.sbuf(ps, f"tp_s_{which}", [128, 11 * D8], BF16)
            absw = c.sbuf(ps, f"tp_a_{which}", [128, 11 * D8], F32)
            NW = 10 * D8
        else:
            wnat = [c.sbuf(ps, f"tp_w{i}_{which}", [128, 4 * H8], F32)
                    for i in range(2)]
            tnat = c.sbuf(ps, f"tp_t_{which}", [128, 4 * H8], BF16)
            msk = c.sbuf(ps, f"tp_m_{which}", [128, 4 * H8], BF16)
            sgn = c.sbuf(ps, f"tp_s_{which}", [128, 4 * H8], BF16)
            absw = c.sbuf(ps, f"tp_a_{which}", [128, 4 * H8], F32)
        ptr = [c.psum(ps, f"tp_ptr{i}_{which}", [128, 2048], BF16)
               for i in range(2)]
        dv0, ac0, gp0, pe0 = c.dv.n, c.ac.n, c.gp.n, c.pe.n
        qa = [c.q[0], c.q[1]]
        qa_b = [q.n for q in qa]
        if not down:
            DVJ, ACJ, PEJ = 4 + 8, 4, 4
        else:
            DVJ, ACJ, PEJ = 2 + 11, 2, 11

        with nc.Block() as blk:
            def b_sync(E):
                for j in range(8):
                    if j >= 2:
                        E.wait(c.dv, dv0 + DVJ * (j - 1))
                    if not down:
                        E.dma(qa[j % 2], wnat[j % 2][:],
                              wsrc[j, 0:1280, :].rearrange(
                                  "(oc p) d -> p oc d", p=128))
                        E.dma(qa[j % 2], wn86[j % 2][0:LAST_OC, :],
                              wsrc[j, 1280:H8, :])
                    else:
                        E.dma(qa[j % 2], wnat[j % 2][:],
                              wsrc[j].rearrange("(oc p) d -> p oc d", p=128))
            c.sync_body(blk, b_sync)

            def b_dve(E):
                for j in range(8):
                    E.wait(qa[j % 2], qa_b[j % 2] + 16 * (2 - down) * (j // 2 + 1))
                    E.wait(c.ac, ac0 + ACJ * j + (2 if not down else 1))
                    if j >= 1:
                        E.wait(c.pe, pe0 + PEJ * j)
                    if not down:
                        E.tensor_single_scalar(
                            out=msk[:, 0:NW], in_=absw[:, 0:NW],
                            scalar=t["scale_half"][:, sidx:sidx + 1],
                            op=ALU.is_gt)
                        E.tensor_single_scalar(
                            out=msk[0:LAST_OC, NW:], in_=absw[0:LAST_OC, NW:],
                            scalar=t["scale_half"][0:LAST_OC, sidx:sidx + 1],
                            op=ALU.is_gt)
                        E.wait(c.ac, ac0 + ACJ * (j + 1))
                        E.tensor_tensor(out=tnat[:, 0:NW], in0=msk[:, 0:NW],
                                        in1=sgn[:, 0:NW], op=ALU.mult)
                        E.tensor_tensor(out=tnat[0:LAST_OC, NW:],
                                        in0=msk[0:LAST_OC, NW:],
                                        in1=sgn[0:LAST_OC, NW:], op=ALU.mult,
                                        dep=False)
                    else:
                        E.tensor_single_scalar(
                            out=msk[:], in_=absw[:],
                            scalar=t["scale_half"][:, sidx:sidx + 1],
                            op=ALU.is_gt)
                        E.wait(c.ac, ac0 + ACJ * (j + 1))
                        E.tensor_tensor(out=tnat[:], in0=msk[:], in1=sgn[:],
                                        op=ALU.mult)
                    nev = 8 if not down else 11
                    for ev in range(nev):
                        if not down:
                            dcc, half = divmod(ev, 2)
                            E.wait(c.pe, pe0 + PEJ * j + dcc + 1)
                            base = (j * 4 + dcc) * H8
                            if half == 0:
                                E.tensor_copy(out=tT[:, base:base + 1024],
                                              in_=ptr[(j * 4 + dcc) % 2][:, 0:1024],
                                              dep=False)
                            else:
                                E.tensor_copy(out=tT[:, base + 1024:base + H8],
                                              in_=ptr[(j * 4 + dcc) % 2][:, 1024:1366],
                                              dep=False)
                        else:
                            hc = ev
                            E.wait(c.pe, pe0 + PEJ * j + hc + 1)
                            base = (j * 11 + hc) * D8
                            hp = 128 if hc < 10 else LAST_OC
                            E.tensor_copy(out=tT[0:hp, base:base + D8],
                                          in_=ptr[(j * 11 + hc) % 2][0:hp, 0:D8],
                                          dep=False)
            c.dve_body(blk, b_dve)

            def b_act(E):
                for j in range(8):
                    E.wait(qa[j % 2], qa_b[j % 2] + 16 * (2 - down) * (j // 2 + 1))
                    if j >= 1:
                        E.wait(c.dv, dv0 + DVJ * (j - 1) + (4 if not down else 2))
                    if not down:
                        E.activation(out=absw[:, 0:NW], in_=wnat[j % 2][:],
                                     func=AF.Abs)
                        E.activation(out=absw[0:LAST_OC, NW:],
                                     in_=wn86[j % 2][0:LAST_OC, :], func=AF.Abs)
                        E.sign(out=sgn[:, 0:NW], in_=wnat[j % 2][:])
                        E.sign(out=sgn[0:LAST_OC, NW:], in_=wn86[j % 2][0:LAST_OC, :])
                    else:
                        E.activation(out=absw[:], in_=wnat[j % 2][:], func=AF.Abs)
                        E.sign(out=sgn[:], in_=wnat[j % 2][:])
            c.act_body(blk, b_act)


            def b_pe(E):
                e = E.e
                for j in range(8):
                    E.wait(c.dv, dv0 + DVJ * j + (4 if not down else 2))
                    if not down:
                        for dcc in range(4):
                            gidx = j * 4 + dcc
                            if gidx >= 2:
                                E.wait(c.dv, dv0 + DVJ * ((gidx - 2) // 4) + 4 +
                                       2 * ((gidx - 2) % 4) + 2)
                            for oc in range(11):
                                if oc < 10:
                                    in_ap = tnat[:, oc * D8 + dcc * 128:
                                                 oc * D8 + (dcc + 1) * 128]
                                else:
                                    in_ap = tnat[0:LAST_OC,
                                                 NW + dcc * 128:NW + (dcc + 1) * 128]
                                w = 128 if oc < 10 else LAST_OC
                                idw = t["id_bf"][:] if oc < 10 else \
                                    t["id_bf"][0:LAST_OC, 0:LAST_OC]
                                ins = e.transpose(
                                    out=ptr[gidx % 2][:, oc * 128:oc * 128 + w],
                                    in_=in_ap, identity=idw)
                            c.pe.inc(ins)
                    else:
                        for hc in range(11):
                            gidx = j * 11 + hc
                            if gidx >= 2:
                                E.wait(c.dv, dv0 + DVJ * ((gidx - 2) // 11) + 2 +
                                       ((gidx - 2) % 11) + 1)
                            hp = 128 if hc < 10 else LAST_OC
                            for oc in range(4):
                                in_ap = tnat[:, oc * H8 + hc * 128:
                                             oc * H8 + hc * 128 + hp]
                                ins = e.transpose(
                                    out=ptr[gidx % 2][0:hp, oc * 128:(oc + 1) * 128],
                                    in_=in_ap, identity=t["id_bf"][:])
                            c.pe.inc(ins)
            c.pe_body(blk, b_pe)


def _gate_sig(k):
    items = [(i, JOF[i, k], SGN[i, k]) for i in range(8)]
    last_pos = max(n for n, it in enumerate(items) if it[2] > 0)
    last_neg = max(n for n, it in enumerate(items) if it[2] < 0)
    return items, last_pos, last_neg


def phase_ffn_gateup(c, which):
    nc = c.nc
    t = c.t
    outer = contextlib.ExitStack()
    tT = c.sbuf(outer, f"tT_{which}", [128, 8 * 4 * H8], BF16)
    _emit_tprep(c, outer, which, tT)
    up = (which == "up")
    with contextlib.ExitStack() as ps:
        ln3h = c.sbuf(ps, f"g_ln3h_{which}", [128, FC * 512], BF16)
        ln3l = c.sbuf(ps, f"g_ln3l_{which}", [128, FC * 512], BF16)
        ev = [c.sbuf(ps, f"g_ev{i}_{which}", [128, 512], F32) for i in range(2)]
        pre = [c.sbuf(ps, f"g_pre{i}_{which}", [128, 512], F32) for i in range(2)]
        res = [c.sbuf(ps, f"g_res{i}_{which}", [128, 512], F32) for i in range(2)]
        if up:
            sgld = [c.sbuf(ps, f"g_sg{i}", [128, 512], F32) for i in range(2)]
            hh = [c.sbuf(ps, f"g_hh{i}", [128, 512], BF16) for i in range(2)]
            hl = [c.sbuf(ps, f"g_hl{i}", [128, 512], BF16) for i in range(2)]
        pp = [c.psum(ps, f"g_pp{i}_{which}", [128, 512], F32) for i in range(2)]
        pn = [c.psum(ps, f"g_pn{i}_{which}", [128, 512], F32) for i in range(2)]

        q0 = c.q[0]
        qsg = [c.q[2], c.q[3]]
        qst = [c.q[4], c.q[5], c.q[6], c.q[7]]
        with nc.Block() as blk:
            dv0, ac0, gp0, pe0 = c.dv.n, c.ac.n, c.gp.n, c.pe.n
            q0b = q0.n
            qsg_b = [q.n for q in qsg]
            qst_b = [q.n for q in qst]
            ACG = 2
            DVG = 2 if up else 1
            GPG = 1 if up else 0

            def b_sync(E):
                for s in range(2):
                    if s:
                        E.wait(c.pe, pe0 + 88)
                    E.dma(q0, ln3h[:], c.d["ln3fm_hi"][s].rearrange(
                        "p a b t -> p (a b t)"))
                    E.dma(q0, ln3l[:], c.d["ln3fm_lo"][s].rearrange(
                        "p a b t -> p (a b t)"))
            c.sync_body(blk, b_sync)

            def b_scalar(E):
                for s in range(2):
                    for g in range(88):
                        k, oc = divmod(g, 11)
                        gg = s * 88 + g
                        par = gg % 2
                        w = 128 if oc < 10 else LAST_OC
                        E.wait(c.pe, pe0 + gg + 1)
                        if up:
                            E.dma(qsg[par], sgld[par][0:w, :],
                                  c.d["sg"][s, g, 0:w, :])
                        E.activation(out=ev[par][0:w, :], in_=pp[par][0:w, :],
                                     func=AF.Copy)
                        E.wait(c.dv, dv0 + DVG * gg + 1)
                        if not up:
                            if gg >= 2:
                                E.wait(qst[par], qst_b[par] + 16 * (gg // 2))
                            E.activation(out=res[par][0:w, :],
                                         in_=pre[par][0:w, :], func=AF.Silu,
                                         scale=t["scale_half"][0:w, 4:5])
                            E.wait(c.ac, c.ac.n)
                            E.dma(qst[par], c.d["sg"][s, g, 0:w, :],
                                  res[par][0:w, :])
                        else:
                            E.wait(c.gp, gp0 + GPG * gg + 1)
                            if gg >= 2:
                                E.wait(qst[2 * par],
                                       qst_b[2 * par] + 16 * (gg // 2))
                            E.activation(out=hh[par][0:w, :],
                                         in_=res[par][0:w, :], func=AF.Identity,
                                         scale=t["scale_half"][0:w, 5:6])
                            E.wait(c.dv, dv0 + DVG * gg + 2)
                            E.dma(qst[2 * par], c.d["h_hi"][s, g, 0:w, :],
                                  hh[par][0:w, :])
                            E.dma(qst[2 * par + 1], c.d["h_lo"][s, g, 0:w, :],
                                  hl[par][0:w, :])
            c.act_body(blk, b_scalar)

            def b_dve(E):
                for s in range(2):
                    for g in range(88):
                        gg = s * 88 + g
                        par = gg % 2
                        oc = g % 11
                        w = 128 if oc < 10 else LAST_OC
                        E.wait(c.ac, ac0 + ACG * gg + 1)
                        E.tensor_tensor(out=pre[par][0:w, :], 
                                        in0=ev[par][0:w, :],
                                        in1=pn[par][0:w, :], op=ALU.subtract)
                        if up:
                            E.wait(c.gp, gp0 + gg + 1)
                            E.wait(c.ac, ac0 + ACG * gg + 2)
                            if gg >= 2:
                                E.wait(qst[2 * par + 1],
                                       qst_b[2 * par + 1] + 16 * (gg // 2))
                            E.scalar_tensor_tensor(
                                out=hl[par][0:w, :], in0=res[par][0:w, :],
                                scalar=t["scale_half"][0:w, 5:6],
                                in1=hh[par][0:w, :],
                                op0=ALU.mult, op1=ALU.subtract)
            c.dve_body(blk, b_dve)

            if up:
                def b_pool(E):
                    for s in range(2):
                        for g in range(88):
                            gg = s * 88 + g
                            par = gg % 2
                            oc = g % 11
                            w = 128 if oc < 10 else LAST_OC
                            E.wait(c.dv, dv0 + DVG * gg + 1)
                            E.wait(qsg[par],
                                   qsg_b[par] + 16 * (gg // 2 + 1))
                            if gg >= 2:
                                E.wait(c.ac, ac0 + ACG * (gg - 2) + 2)
                            E.tensor_tensor(out=res[par][0:w, :],
                                            in0=pre[par][0:w, :],
                                            in1=sgld[par][0:w, :],
                                            op=ALU.mult)
                c.pool_body(blk, b_pool)

            def b_pe(E):
                e = E.e
                for s in range(2):
                    E.wait(q0, q0b + 32 * (s + 1))
                    for g in range(88):
                        k, oc = divmod(g, 11)
                        gg = s * 88 + g
                        par = gg % 2
                        w = 128 if oc < 10 else LAST_OC
                        if gg >= 2:
                            E.wait(c.ac, ac0 + ACG * (gg - 2) + 1)
                            E.wait(c.dv, dv0 + DVG * (gg - 2) + 1)
                        items, last_pos, last_neg = _gate_sig(k)
                        stp = {1: False, -1: False}
                        for n, (i, j, sg_) in enumerate(items):
                            dst = pp[par] if sg_ > 0 else pn[par]
                            first = not stp[sg_]
                            stp[sg_] = True
                            last = (n == (last_pos if sg_ > 0 else last_neg))
                            for dcc in range(4):
                                lhsT = tT[:, (j * 4 + dcc) * H8 + oc * 128:
                                          (j * 4 + dcc) * H8 + oc * 128 + w]
                                dc = i * 4 + dcc
                                rh = ln3h[:].rearrange(
                                    "p (a b t) -> p a b t", a=4, b=FC)[:, :, dc, :]
                                rl = ln3l[:].rearrange(
                                    "p (a b t) -> p a b t", a=4, b=FC)[:, :, dc, :]
                                e.matmul(out=dst[0:w, :], lhsT=lhsT, rhs=rh,
                                         start=(first and dcc == 0),
                                         stop=False)
                                ins = e.matmul(out=dst[0:w, :], lhsT=lhsT,
                                               rhs=rl, start=False,
                                               stop=(last and dcc == 3))
                        c.pe.inc(ins)
            c.pe_body(blk, b_pe)
    outer.close()


def phase_ffn_down(c):
    nc = c.nc
    t = c.t
    outer = contextlib.ExitStack()
    tT = c.sbuf(outer, "tT_down", [128, 8 * 11 * D8], BF16)
    _emit_tprep(c, outer, "down", tT)
    OCB = [list(range(b, min(b + 3, FC))) for b in range(0, FC, 3)]
    with contextlib.ExitStack() as ps:
        hld_h = [c.sbuf(ps, f"d_hh{i}", [128, 11 * 512], BF16) for i in range(2)]
        hld_l = [c.sbuf(ps, f"d_hl{i}", [128, 11 * 512], BF16) for i in range(2)]
        ev = c.sbuf(ps, "d_ev", [128, 512], F32)
        pre = c.sbuf(ps, "d_pre", [128, 512], F32)
        f_tr = [c.sbuf(ps, f"d_tr{i}", [128, 512], F32) for i in range(2)]
        pp = [c.psum(ps, f"d_pp{i}", [128, 512], F32) for i in range(3)]
        pn = [c.psum(ps, f"d_pn{i}", [128, 512], F32) for i in range(3)]
        ps_trf = c.psum(ps, "d_ptrf", [128, 512], F32)

        qh = [c.q[2], c.q[3]]
        qst = c.qsw[8:12]
        with nc.Block() as blk:
            dv0, ac0, pe0 = c.dv.n, c.ac.n, c.pe.n
            qh_b = [q.n for q in qh]
            qst_b = [q.n for q in qst]
            NB = len(OCB)
            def pe_blk(si, b):
                return pe0 + si * (8 * NB + FC) + b * 8 + sum(
                    len(OCB[x]) for x in range(b))

            pe_sched = []
            n = 0
            for si in range(2):
                for b in range(NB):
                    n += 8
                    n += len(OCB[b])
                    pe_sched.append(n)

            def pe_after(si, b):
                return pe0 + pe_sched[si * NB + b]

            def pe_after_loads(si, b):
                return pe0 + pe_sched[si * NB + b] - len(OCB[b])

            def b_sync(E):
                for si in range(2):
                    for b in range(NB):
                        for i in range(8):
                            gi = (si * NB + b) * 8 + i
                            par = gi % 2
                            if gi >= 2:
                                pb, pi = divmod(gi - 2, 8)
                                E.wait(c.pe, pe0 + (pe_sched[pb - 1] if pb else 0)
                                       + pi + 1)
                            E.dma(qh[par],
                                  hld_h[par][:].rearrange("p (g t) -> p g t",
                                                          g=11),
                                  c.d["h_hi"][si].rearrange(
                                      "g p t -> p g t")[:, i * 11:(i + 1) * 11, :])
                            E.dma(qh[par],
                                  hld_l[par][:].rearrange("p (g t) -> p g t",
                                                          g=11),
                                  c.d["h_lo"][si].rearrange(
                                      "g p t -> p g t")[:, i * 11:(i + 1) * 11, :])
            c.sync_body(blk, b_sync)

            def b_pe(E):
                e = E.e
                evac_n = 0
                for si in range(2):
                    for b in range(NB):
                        blkbase = pe0 + (pe_sched[si * NB + b - 1]
                                         if (si * NB + b) else 0)
                        started = {}
                        for i in range(8):
                            gi = (si * NB + b) * 8 + i
                            par = gi % 2
                            E.wait(qh[par], qh_b[par] + 32 * (gi // 2 + 1))
                            if i == 0:
                                if si * NB + b >= 1:
                                    E.wait(c.ac, ac0 + evac_n)
                                    E.wait(c.dv, dv0 + 2 * evac_n)
                            for hc in range(11):
                                hp = 128 if hc < 10 else LAST_OC
                                for gsl, ocd in enumerate(OCB[b]):
                                    k, olocal = divmod(ocd, 4)
                                    j = JOF[i, k]
                                    sg_ = SGN[i, k]
                                    dst = pp[gsl] if sg_ > 0 else pn[gsl]
                                    key = (gsl, sg_ > 0)
                                    first = key not in started
                                    started[key] = True
                                    lhsT = tT[0:hp, (j * 11 + hc) * D8 +
                                              olocal * 128:
                                              (j * 11 + hc) * D8 +
                                              olocal * 128 + 128]
                                    rh = hld_h[par][0:hp, hc * 512:(hc + 1) * 512]
                                    rl = hld_l[par][0:hp, hc * 512:(hc + 1) * 512]
                                    li = max(i2 for i2 in range(8)
                                             if (SGN[i2, k] > 0) == (sg_ > 0))
                                    last = (i == li and hc == 10)
                                    e.matmul(out=dst[:], lhsT=lhsT, rhs=rh,
                                             start=first, stop=False)
                                    ins = e.matmul(out=dst[:], lhsT=lhsT,
                                                   rhs=rl, start=False,
                                                   stop=last)
                            c.pe.inc(ins)
                        for gsl, ocd in enumerate(OCB[b]):
                            E.wait(c.ac, ac0 + evac_n + 1)
                            E.wait(c.dv, dv0 + 2 * evac_n + 1)
                            if evac_n >= 1:
                                E.wait(c.dv, dv0 + 2 * (evac_n - 1) + 2)
                            for tt2 in range(4):
                                ins = e.transpose(
                                    out=ps_trf[:, tt2 * 128:(tt2 + 1) * 128],
                                    in_=pre[:, tt2 * 128:(tt2 + 1) * 128],
                                    identity=t["id_f32"][:])
                            c.pe.inc(ins)
                            evac_n += 1
            c.pe_body(blk, b_pe)

            def b_act(E):
                evac_n = 0
                for si in range(2):
                    for b in range(NB):
                        blkbase = pe0 + (pe_sched[si * NB + b - 1]
                                         if (si * NB + b) else 0)
                        for gsl, ocd in enumerate(OCB[b]):
                            E.wait(c.pe, blkbase + 8)
                            if evac_n >= 1:
                                E.wait(c.dv, dv0 + 2 * (evac_n - 1) + 1)
                            E.activation(out=ev[:], in_=pp[gsl][:],
                                         func=AF.Copy)
                            evac_n += 1
            c.act_body(blk, b_act)

            def b_pool2(E):
                evac_n = 0
                for si in range(2):
                    for b in range(NB):
                        for gsl, ocd in enumerate(OCB[b]):
                            E.wait(c.dv, dv0 + 2 * evac_n + 2)
                            for tt2 in range(4):
                                E.dma(qst[tt2],
                                      c.d["ffn_tm"][si * 4 + tt2, ocd],
                                      f_tr[evac_n % 2][:, tt2 * 128:
                                                       (tt2 + 1) * 128])
                            evac_n += 1
            c.pool_body(blk, b_pool2)

            def b_dve(E):
                evac_n = 0
                for si in range(2):
                    for b in range(NB):
                        blkbase = pe0 + (pe_sched[si * NB + b - 1]
                                         if (si * NB + b) else 0)
                        for gsl, ocd in enumerate(OCB[b]):
                            E.wait(c.ac, ac0 + evac_n + 1)
                            E.tensor_tensor(out=pre[:], in0=ev[:],
                                            in1=pn[gsl][:], op=ALU.subtract)
                            E.wait(c.pe, blkbase + 8 + gsl + 1)
                            if evac_n >= 2:
                                for x in range(4):
                                    E.wait(qst[x], qst_b[x] + 16 * (evac_n - 1))
                            E.tensor_scalar_mul(
                                out=f_tr[evac_n % 2][:], in0=ps_trf[:],
                                scalar1=t["scale_half"][:, 6:7])
                            evac_n += 1
            c.dve_body(blk, b_dve)
    outer.close()



def phase_out(c):
    nc = c.nc
    t = c.t
    with contextlib.ExitStack() as ps:
        f_t = [c.sbuf(ps, f"o_f{i}", [128, D], F32) for i in range(2)]
        x1b = [c.sbuf(ps, f"o_x1{i}", [128, D], F32) for i in range(2)]
        x2_t = [c.sbuf(ps, f"o_x2{i}", [128, D], F32) for i in range(2)]
        lnf4 = c.sbuf(ps, "o_lnf", [128, D], F32)
        stat4 = c.sbuf(ps, "o_stat", [128, 56], F32)
        qf = [c.q[0], c.q[1]]
        qx = [c.q[2], c.q[3]]
        qout = [c.q[4], c.q[5]]
        with nc.Block() as blk:
            dv0, ac0 = c.dv.n, c.ac.n
            qf_b = [q.n for q in qf]
            qx_b = [q.n for q in qx]
            qout_b = [q.n for q in qout]
            DVT = 19
            DVT = 18
            ACT_T = 2

            def b_sync(E):
                for tg in range(8):
                    par = tg % 2
                    if tg >= 2:
                        E.wait(c.dv, dv0 + DVT * (tg - 2) + 18)
                    E.dma(qf[par],
                          f_t[par][:].rearrange("p (m t) -> p m t", m=FC),
                          c.d["ffn_tm"][tg].rearrange("m p t -> p m t"))
                    E.dma(qx[par], x1b[par][:],
                          c.d["x1"][tg * 128:(tg + 1) * 128, :])
                    E.wait(c.dv, dv0 + DVT * tg + 18)
                    E.dma(qout[par], c.d["out"][tg * 128:(tg + 1) * 128, :],
                          x2_t[par][:])
                for q in (qout[0], qout[1]):
                    E.wait(q, q.n)
            c.sync_body(blk, b_sync)

            def b_dve(E):
                for tg in range(8):
                    par = tg % 2
                    E.wait(qf[par], qf_b[par] + 16 * (tg // 2 + 1))
                    _ln_recipe_dve(E, c, f_t[par][:], stat4, ac0 + ACT_T * tg + 2)
                    E.tensor_scalar(out=lnf4[:], in0=f_t[par][:],
                                    scalar1=stat4[:, 53:54],
                                    scalar2=stat4[:, 54:55],
                                    op0=ALU.mult, op1=ALU.add)
                    E.wait(qx[par], qx_b[par] + 16 * (tg // 2 + 1))
                    if tg >= 2:
                        E.wait(qout[par], qout_b[par] + 16 * (tg // 2))
                    E.tensor_tensor(out=x2_t[par][:], in0=lnf4[:],
                                    in1=x1b[par][:], op=ALU.add)
            c.dve_body(blk, b_dve)

            def b_act(E):
                for tg in range(8):
                    E.wait(c.dv, dv0 + DVT * tg + 10)
                    E.activation(out=stat4[:, 51:52], in_=stat4[:, 50:51],
                                 func=AF.Ln)
                    E.activation(out=stat4[:, 51:52], in_=stat4[:, 51:52],
                                 func=AF.Exp, scale=-0.5)
            c.act_body(blk, b_act)


def build_program():
    nc = bass.Bass("TRN2", target_bir_lowering=False)
    es = contextlib.ExitStack()
    c = Ctx(nc, es)
    for s in ("pe", "dv", "ac", "gp", "bar", "dv2"):
        c.sem(s)
    c.q = [c.sem(f"q{i}") for i in range(16)]
    c.qsw = [c.sem(f"qsw{i}") for i in range(12)]
    declare_io(c)

    phases = [phase_init, phase_wprep, phase_quarters, phase_kvfin,
              lambda c: phase_attn_a(c, 0), lambda c: phase_attn_b(c, 0),
              lambda c: phase_attn_a(c, 1), lambda c: phase_attn_b(c, 1),
              lambda c: phase_ffn_gateup(c, "gate"),
              lambda c: phase_ffn_gateup(c, "up"),
              phase_ffn_down, phase_out]
    for i, ph in enumerate(phases):
        if i >= N_PHASES:
            break
        ph(c)
        barrier(c)
    with nc.Block() as blk:
        def b_flush(e):
            for q in c.q + c.qsw:
                if q.n:
                    e.wait_ge(q.h, q.n)
        blk.sync(b_flush)
    es.close()
    return nc


def kernel(**inputs):
    x = np.ascontiguousarray(inputs["L0_attn_hidden_states"])
    ident = np.eye(128, dtype=np.float32)
    base = {
        "Wq": inputs["L0_attn_Wq"], "Wk": inputs["L0_attn_Wk"],
        "Wv": inputs["L0_attn_Wv"], "Wo": inputs["L0_attn_Wo"],
        "bq": inputs["L0_attn_bq"], "bk": inputs["L0_attn_bk"],
        "bv": inputs["L0_attn_bv"], "bo": inputs["L0_attn_bo"],
        "w_gate": inputs["L0_ffn_w_gate"], "w_up": inputs["L0_ffn_w_up"],
        "w_down": inputs["L0_ffn_w_down"], "ident": ident,
    }
    base = {k: np.ascontiguousarray(v) for k, v in base.items()}
    in_maps = []
    for core in range(N_CORES):
        b, hf = core // 2, core % 2
        own = x[b, hf * 1024:(hf + 1) * 1024]
        oth = x[b, (1 - hf) * 1024:(2 - hf) * 1024]
        m = dict(base)
        m["x"] = np.concatenate([own, oth], axis=0)
        in_maps.append(m)
    nc = build_program()
    res = run_bass_kernel_spmd(nc, in_maps, core_ids=list(range(N_CORES)))
    y = np.empty((4, SEQ, D), np.float32)
    for core in range(N_CORES):
        b, hf = core // 2, core % 2
        y[b, hf * 1024:(hf + 1) * 1024] = res.results[core]["out"]
    return y


# version 16
# speedup vs baseline: 1.0482x; 1.0482x over previous
import contextlib
import numpy as np
import ml_dtypes
import concourse.bass as bass
import concourse.mybir as mybir
from concourse.bass_utils import run_bass_kernel_spmd

F32 = mybir.dt.float32
I32 = mybir.dt.int32
BF16 = mybir.dt.bfloat16
AF = mybir.ActivationFunctionType
ALU = mybir.AluOpType
AX = mybir.AxisListType

SEQ, OWN, D, DKV = 2048, 1024, 4096, 1024
FC = D // 128
KC = DKV // 128
NH, NKV, HD = 32, 8, 128
H8, D8, HID = 1366, 512, 10928
NOC = 11
LAST_OC = H8 - 10 * 128
FLAT8 = (8 * H8 * D8) // 128
EPS_LN = 1e-5
EPS_AT = 1e-10
N_CORES = 8


def _oct_tables():
    TR = [(1, 2, 3), (1, 4, 5), (1, 7, 6), (2, 4, 6), (2, 5, 7), (3, 4, 7), (3, 6, 5)]
    S = np.zeros((8, 8, 8), np.float32)
    S[0, 0, 0] = 1.0
    for i in range(1, 8):
        S[0, i, i] = 1.0
        S[i, 0, i] = 1.0
        S[i, i, 0] = -1.0
    for a, b, c in TR:
        for x, y, z in ((a, b, c), (b, c, a), (c, a, b)):
            S[x, y, z] = 1.0
            S[y, x, z] = -1.0
    jof = np.zeros((8, 8), int)
    sgn = np.zeros((8, 8), int)
    for i in range(8):
        for k in range(8):
            js = [j for j in range(8) if S[j, i, k] != 0]
            assert len(js) == 1
            jof[i, k] = js[0]
            sgn[i, k] = int(S[js[0], i, k])
    return jof, sgn


JOF, SGN = _oct_tables()

N_PHASES = 99
DEBUG_TAPS = ()


class Sem:
    __slots__ = ("h", "n")

    def __init__(self, h):
        self.h = h
        self.n = 0

    def inc(self, inst, k=1):
        inst.then_inc(self.h, k)
        self.n += k
        return self.n


_ENG_OPS = (
    "memset", "memzero", "tensor_copy", "tensor_tensor", "tensor_add",
    "tensor_sub", "tensor_mul", "tensor_max", "tensor_scalar",
    "tensor_single_scalar", "tensor_scalar_add", "tensor_scalar_sub",
    "tensor_scalar_mul", "tensor_scalar_max", "tensor_scalar_min",
    "tensor_relu", "tensor_reduce", "reduce_sum", "reduce_max",
    "scalar_tensor_tensor", "reciprocal", "activation", "sign", "square",
    "bn_stats", "bn_aggr", "select", "transpose",
    "copy", "add", "mul",
)


class Eng:

    def __init__(self, c, e, sem, qsem=None):
        self.c = c
        self.e = e
        self.s = sem
        self.q = qsem

    def wait(self, sem, val):
        self.e.wait_ge(sem.h, val)

    def dma(self, q, out, in_, **kw):
        if q.n:
            self.e.wait_ge(q.h, q.n)
        ins = self.e.dma_start(out=out, in_=in_, **kw)
        q.inc(ins, 16)
        return q.n

    def nop_inc(self):
        self.s.inc(self.e.nop())
        return self.s.n

    def __getattr__(self, name):
        if name not in _ENG_OPS:
            raise AttributeError(name)
        fn = getattr(self.e, name)

        def call(*a, dep=True, **kw):
            if dep:
                self.e.wait_ge(self.s.h, self.s.n)
            ins = fn(*a, **kw)
            self.s.inc(ins)
            return self.s.n
        return call


class Ctx:
    def __init__(self, nc, es):
        self.nc = nc
        self.es = es
        self.d = {}
        self.t = {}

    def sem(self, name):
        s = Sem(self.es.enter_context(self.nc.semaphore(name)))
        setattr(self, name, s)
        return s

    def sbuf(self, es, name, shape, dt):
        return es.enter_context(self.nc.sbuf_tensor(name, shape, dt))

    def psum(self, es, name, shape, dt=F32):
        return es.enter_context(self.nc.psum_tensor(name, shape, dt))

    def dram(self, name, shape, dt, kind="Internal"):
        if kind == "Internal" and name in DEBUG_TAPS:
            kind = "ExternalOutput"
        ap = self.nc.dram_tensor(name, shape, dt, kind=kind).ap()
        self.d[name] = ap
        return ap

    def sync_body(self, blk, fn):
        blk.sync(lambda e: fn(Eng(self, e, self.pe)))

    def scalarq_body(self, blk, fn):
        blk.scalar(lambda e: fn(Eng(self, e, self.ac)))

    act_body = scalarq_body

    def dve_body(self, blk, fn):
        blk.vector(lambda e: fn(Eng(self, e, self.dv)))

    def pool_body(self, blk, fn):
        blk.gpsimd(lambda e: fn(Eng(self, e, self.gp)))

    def pe_body(self, blk, fn):
        blk.tensor(lambda e: fn(Eng(self, e, self.pe)))


def _rsqrt_dve(E, c, stat2):
    vi = stat2[:, 50:51].bitcast(I32)
    yi = stat2[:, 51:52].bitcast(I32)
    ti = stat2[:, 52:53].bitcast(I32)
    E.tensor_scalar(out=ti, in0=vi, scalar1=1, scalar2=None,
                    op0=ALU.arith_shift_right)
    E.tensor_tensor(out=yi, in0=c.t["rsq_m"][:], in1=ti, op=ALU.subtract)
    for _ in range(3):
        E.tensor_tensor(out=stat2[:, 52:53], in0=stat2[:, 51:52],
                        in1=stat2[:, 51:52], op=ALU.mult)
        E.tensor_tensor(out=stat2[:, 52:53], in0=stat2[:, 52:53],
                        in1=stat2[:, 50:51], op=ALU.mult)
        E.tensor_scalar(out=stat2[:, 52:53], in0=stat2[:, 52:53],
                        scalar1=-0.5, scalar2=1.5, op0=ALU.mult, op1=ALU.add)
        E.tensor_tensor(out=stat2[:, 51:52], in0=stat2[:, 51:52],
                        in1=stat2[:, 52:53], op=ALU.mult)


def barrier(c):
    nc = c.nc
    target = c.bar.n + 5
    with nc.Block() as blk:
        def b_sync(e):
            for q in c.q + c.qsw:
                if q.n:
                    e.wait_ge(q.h, q.n)
            e.nop().then_inc(c.bar.h, 1)
            e.wait_ge(c.bar.h, target)

        def body(e):
            e.nop().then_inc(c.bar.h, 1)
            e.wait_ge(c.bar.h, target)
        blk.sync(b_sync)
        blk.scalar(body)
        blk.vector(body)
        blk.tensor(body)
        blk.gpsimd(body)
    c.bar.n = target


def declare_io(c):
    d = c.dram
    d("x", [SEQ, D], F32, kind="ExternalInput")
    d("Wq", [D, D], F32, kind="ExternalInput")
    d("Wk", [DKV, D], F32, kind="ExternalInput")
    d("Wv", [DKV, D], F32, kind="ExternalInput")
    d("Wo", [D, D], F32, kind="ExternalInput")
    d("bq", [D], F32, kind="ExternalInput")
    d("bk", [DKV], F32, kind="ExternalInput")
    d("bv", [DKV], F32, kind="ExternalInput")
    d("bo", [D], F32, kind="ExternalInput")
    d("w_gate", [8, H8, D8], F32, kind="ExternalInput")
    d("w_up", [8, H8, D8], F32, kind="ExternalInput")
    d("w_down", [8, D8, H8], F32, kind="ExternalInput")
    d("ident", [128, 128], F32, kind="ExternalInput")
    d("out", [OWN, D], F32, kind="ExternalOutput")
    d("wqT_hi", [FC, 128, FC, 128], BF16)
    d("wqT_lo", [FC, 128, FC, 128], BF16)
    d("wkT_hi", [KC, 128, FC, 128], BF16)
    d("wkT_lo", [KC, 128, FC, 128], BF16)
    d("wvT_hi", [KC, 128, FC, 128], BF16)
    d("wvT_lo", [KC, 128, FC, 128], BF16)
    d("woT_hi", [FC, 128, FC, 128], BF16)
    d("woT_lo", [FC, 128, FC, 128], BF16)
    d("ln1fm_hi", [2, 128, FC * 512], BF16)
    d("ln1fm_lo", [2, 128, FC * 512], BF16)
    d("o_tm", [8, FC, 128, 128], F32)
    d("ffn_tm", [8, FC, 128, 128], F32)
    d("x1", [OWN, D], F32)
    d("ln3fm_hi", [2, 128, 4, FC, 128], BF16)
    d("ln3fm_lo", [2, 128, 4, FC, 128], BF16)
    d("sg", [2, 88, 128, 512], F32)
    d("h_hi", [2, 88, 128, 512], BF16)
    d("h_lo", [2, 88, 128, 512], BF16)



def phase_init(c):
    nc = c.nc
    es = c.es
    t = c.t
    t["id_f32"] = c.sbuf(es, "id_f32", [128, 128], F32)
    t["id_bf"] = c.sbuf(es, "id_bf", [128, 128], BF16)
    t["bias_cols"] = c.sbuf(es, "bias_cols", [128, 80], F32)
    t["nbias_cols"] = c.sbuf(es, "nbias_cols", [128, 80], F32)
    t["scale_half"] = c.sbuf(es, "scale_half", [128, 8], F32)
    t["kvt_hi"] = c.sbuf(es, "kvt_hi", [128, DKV], BF16)
    t["kvt_lo"] = c.sbuf(es, "kvt_lo", [128, DKV], BF16)
    t["recip"] = c.sbuf(es, "recip", [128, NKV], F32)
    t["scol"] = c.sbuf(es, "scol", [128, NKV], F32)
    t["ones11"] = c.sbuf(es, "ones11", [1, 1], F32)
    t["rsq_m"] = c.sbuf(es, "rsq_m", [128, 1], I32)

    ps_init = contextlib.ExitStack()
    zpad = c.sbuf(ps_init, "zpad", [42, 16 * 512], BF16)
    brow = c.sbuf(ps_init, "brow", [1, 4 * D + 128], F32)
    red = c.sbuf(ps_init, "red", [128, 5], F32)
    wbuf = [c.sbuf(ps_init, f"sc_wbuf{i}", [128, 4096], F32) for i in range(2)]
    tots = c.sbuf(ps_init, "tots", [1, 4], F32)
    ps_bias = c.psum(ps_init, "ps_bias", [128, 512], F32)
    ps_misc = c.psum(ps_init, "ps_misc", [128, 512], F32)

    W_SPECS = [("bq", D, 0), ("bk", DKV, 32), ("bv", DKV, 40), ("bo", D, 48)]
    n_ck = 11
    G = 3 * n_ck
    dv0, ac0, gp0, pe0 = c.dv.n, c.ac.n, c.gp.n, c.pe.n

    with nc.Block() as blk:
        def b_pool(E):
            E.memset(red[:], 0.0)
            E.memset(red[:, 4:5], 1.0)
            E.memset(t["ones11"][:], 1.0)
            E.memset(brow[0:1, 4 * D:4 * D + 128], 1.0)
            E.memset(t["scale_half"][:], 0.0)
            E.memset(t["rsq_m"][:], 0x5F3759DF)
            E.memset(zpad[:], 0.0)
        c.pool_body(blk, b_pool)

        q0 = c.q[0]
        q0_base = q0.n

        def b_sync(E):
            E.dma(q0, t["id_f32"][:], c.d["ident"])
            for i, (nm, w, _) in enumerate(W_SPECS):
                E.dma(q0, brow[0:1, i * D:i * D + w],
                      c.d[nm].rearrange("(o n) -> o n", o=1))
            E.wait(c.gp, gp0 + 7)
            for nm in ("h_hi", "h_lo"):
                E.dma(q0, c.d[nm].rearrange(
                    "s (k o) p t -> p s k o t", o=NOC)[LAST_OC:128, :, :,
                                                       NOC - 1, :],
                    zpad[:].rearrange("p (s k t) -> p s k t", s=2, k=8))
        c.sync_body(blk, b_sync)

        qw = [c.q[1], c.q[2]]
        qw_base = [q.n for q in qw]

        def b_act(E):
            for wi, nm in enumerate(["w_gate", "w_up", "w_down"]):
                flat = c.d[nm].rearrange("a b c -> (a b c)").rearrange(
                    "(p n) -> p n", p=128)
                for k in range(n_ck):
                    lo = k * 4096
                    hi = min(FLAT8, lo + 4096)
                    g = wi * n_ck + k
                    if g >= 2:
                        E.wait(c.dv, dv0 + 2 * (g - 1))
                    E.dma(qw[g % 2], wbuf[g % 2][:, 0:hi - lo], flat[:, lo:hi])
            E.wait(c.pe, pe0 + 2)
            E.activation(out=tots[0:1, 0:3], in_=ps_misc[0:1, 0:3],
                         func=AF.Copy, scale=0.5 / (8 * H8 * D8))
            E.wait(q0, q0_base + 16)
            E.activation(out=t["id_bf"][:], in_=t["id_f32"][:], func=AF.Copy)
            E.wait(c.pe, pe0 + 3)
            E.wait(c.gp, gp0 + 6)
            E.activation(out=t["scale_half"][:, 0:3], in_=ps_misc[:, 8:11],
                         func=AF.Copy)
        c.act_body(blk, b_act)

        def b_vector(E):
            E.wait(c.gp, gp0 + 6)
            for wi in range(3):
                for k in range(n_ck):
                    g = wi * n_ck + k
                    lo = k * 4096
                    hi = min(FLAT8, lo + 4096)
                    E.wait(qw[g % 2], qw_base[g % 2] + 16 * (g // 2 + 1))
                    E.reduce_sum(out=red[:, 3:4], in_=wbuf[g % 2][:, 0:hi - lo],
                                 axis=AX.X, apply_absolute_value=True)
                    E.tensor_tensor(out=red[:, wi:wi + 1], in0=red[:, wi:wi + 1],
                                    in1=red[:, 3:4], op=ALU.add)
            E.wait(c.pe, pe0 + 1)
            E.tensor_copy(out=t["bias_cols"][:], in_=ps_bias[:, 0:80])
            E.tensor_scalar_mul(out=t["nbias_cols"][:], in0=t["bias_cols"][:],
                                scalar1=-1.0)
            E.wait(c.ac, ac0 + 3)
            E.tensor_scalar_mul(out=t["scale_half"][:, 4:7],
                                in0=t["scale_half"][:, 0:3], scalar1=2.0)
        c.dve_body(blk, b_vector)

        def b_tensor(E):
            e = E.e
            E.wait(q0, q0_base + 16 * 5)
            E.wait(c.gp, gp0 + 6)
            n = 0
            for i, (nm, w, base) in enumerate(W_SPECS):
                for ch in range(w // 128):
                    ins = e.matmul(out=ps_bias[:, base + ch:base + ch + 1],
                                   lhsT=brow[0:1, i * D + ch * 128:i * D + ch * 128 + 128],
                                   rhs=t["ones11"][:],
                                   start=(n == 0), stop=(n == 79))
                    n += 1
            c.pe.inc(ins)
            E.wait(c.dv, dv0 + 2 * G)
            for wi in range(3):
                ins = e.matmul(out=ps_misc[0:1, wi:wi + 1],
                               lhsT=red[:, wi:wi + 1], rhs=red[:, 4:5],
                               start=(wi == 0), stop=(wi == 2))
            c.pe.inc(ins)
            E.wait(c.ac, ac0 + 1)
            for wi in range(3):
                ins = e.matmul(out=ps_misc[:, 8 + wi:9 + wi],
                               lhsT=brow[0:1, 4 * D:4 * D + 128],
                               rhs=tots[0:1, wi:wi + 1],
                               start=(wi == 0), stop=(wi == 2))
            c.pe.inc(ins)
        c.pe_body(blk, b_tensor)

    assert c.dv.n == dv0 + 2 * G + 3
    assert c.pe.n == pe0 + 3
    assert c.ac.n == ac0 + 3
    ps_init.close()



def phase_wprep(c):
    nc = c.nc
    t = c.t
    glist = []
    for nm, nmc in (("Wk", KC), ("Wv", KC), ("Wq", FC), ("Wo", FC)):
        for mc in range(nmc):
            glist.append((nm, mc))
    G = len(glist)

    with contextlib.ExitStack() as ps:
        nat = [c.sbuf(ps, f"wp_nat{i}", [128, D], F32) for i in range(2)]
        hib = [c.sbuf(ps, f"wp_hi{i}", [128, D], BF16) for i in range(2)]
        lob = [c.sbuf(ps, f"wp_lo{i}", [128, D], BF16) for i in range(2)]
        asm_h = [c.sbuf(ps, f"wp_ah{i}", [128, D], BF16) for i in range(2)]
        asm_l = [c.sbuf(ps, f"wp_al{i}", [128, D], BF16) for i in range(2)]
        ptr = [c.psum(ps, f"wp_ptr{i}", [128, 1024], BF16) for i in range(4)]

        gp0, dv0, ac0, pe0 = c.gp.n, c.dv.n, c.ac.n, c.pe.n
        qn = [c.q[3], c.q[4]]
        qn_base = [q.n for q in qn]
        qsh = [c.q[5], c.q[6]]
        qsl = [c.q[7], c.q[8]]
        qsh_base = [q.n for q in qsh]
        qsl_base = [q.n for q in qsl]

        with nc.Block() as blk:
            def b_sync(E):
                for g, (nm, mc) in enumerate(glist):
                    if g >= 2:
                        E.wait(c.dv, dv0 + 5 * (g - 2) + 1)
                    E.dma(qn[g % 2], nat[g % 2][:],
                          c.d[nm][mc * 128:(mc + 1) * 128, :])
            c.sync_body(blk, b_sync)

            def b_vector(E):
                for g in range(G):
                    E.wait(c.ac, ac0 + 5 * g + 1)
                    if g >= 2:
                        E.wait(c.pe, pe0 + 8 * (g - 1))
                    E.tensor_tensor(out=lob[g % 2][:], in0=nat[g % 2][:],
                                    in1=hib[g % 2][:], op=ALU.subtract)
                    for bb in range(4, 8):
                        gb = g * 8 + bb
                        E.wait(c.pe, pe0 + gb + 1)
                        if bb == 4 and g >= 2:
                            E.wait(qsl[g % 2], qsl_base[g % 2] + 16 * (g // 2))
                        E.tensor_copy(
                            out=asm_l[g % 2][:, (bb % 4) * 1024:(bb % 4 + 1) * 1024],
                            in_=ptr[gb % 4][:], dep=False)
            c.dve_body(blk, b_vector)

            def b_tensor(E):
                e = E.e
                for g in range(G):
                    for bb in range(8):
                        src_ = hib[g % 2] if bb < 4 else lob[g % 2]
                        if bb == 0:
                            E.wait(c.ac, ac0 + 5 * g + 1)
                        if bb == 4:
                            E.wait(c.dv, dv0 + 5 * g + 1)
                        gb = g * 8 + bb
                        if gb >= 4:
                            pb = gb - 4
                            pg, pbb = divmod(pb, 8)
                            if pbb < 4:
                                E.wait(c.ac, ac0 + 5 * pg + 1 + pbb + 1)
                            else:
                                E.wait(c.dv, dv0 + 5 * pg + 1 + (pbb - 4) + 1)
                        for tt in range(8):
                            dc = (bb % 4) * 8 + tt
                            ins = e.transpose(
                                out=ptr[gb % 4][:, tt * 128:(tt + 1) * 128],
                                in_=src_[:, dc * 128:(dc + 1) * 128],
                                identity=t["id_bf"][:])
                        c.pe.inc(ins)
            c.pe_body(blk, b_tensor)

            def b_act(E):
                for g, (nm, mc) in enumerate(glist):
                    E.wait(qn[g % 2], qn_base[g % 2] + 16 * (g // 2 + 1))
                    if g >= 2:
                        E.wait(c.pe, pe0 + 8 * (g - 1))
                    E.activation(out=hib[g % 2][:], in_=nat[g % 2][:],
                                 func=AF.Copy)
                    for bb in range(4):
                        gb = g * 8 + bb
                        E.wait(c.pe, pe0 + gb + 1)
                        if bb == 0 and g >= 2:
                            E.wait(qsh[g % 2], qsh_base[g % 2] + 16 * (g // 2))
                        E.activation(
                            out=asm_h[g % 2][:, bb * 1024:(bb + 1) * 1024],
                            in_=ptr[gb % 4][:], func=AF.Copy, dep=False)
                    E.wait(c.ac, c.ac.n)
                    E.dma(qsh[g % 2], c.d["w" + nm[1:].lower() + "T_hi"][mc],
                          asm_h[g % 2][:])
                    E.wait(c.dv, dv0 + 5 * g + 5)
                    E.dma(qsl[g % 2], c.d["w" + nm[1:].lower() + "T_lo"][mc],
                          asm_l[g % 2][:])
            c.act_body(blk, b_act)

        assert c.pe.n == pe0 + 8 * G
        assert c.ac.n == ac0 + 5 * G
        assert c.dv.n == dv0 + 5 * G



def phase_quarters(c):
    nc = c.nc
    t = c.t
    kvscope = contextlib.ExitStack()
    ps_kv = c.psum(kvscope, "ps_kv", [128, 1024], F32)
    c._kv_scope = kvscope
    c._ps_kv = ps_kv

    ps = contextlib.ExitStack()
    x0t = [c.sbuf(ps, f"x0t{i}", [128, D], F32) for i in range(2)]
    un_t = c.sbuf(ps, "un_t", [128, D], F32)
    ln1h_t = c.sbuf(ps, "ln1h_t", [128, D], BF16)
    ln1l_t = c.sbuf(ps, "ln1l_t", [128, D], BF16)
    stat = c.sbuf(ps, "stat", [128, 56], F32)
    lnfm_h = c.sbuf(ps, "lnfm_h", [128, FC * 512], BF16)
    lnfm_l = c.sbuf(ps, "lnfm_l", [128, FC * 512], BF16)
    wtb_h = [c.sbuf(ps, f"wtb_h{i}", [128, D], BF16) for i in range(2)]
    wtb_l = [c.sbuf(ps, f"wtb_l{i}", [128, D], BF16) for i in range(2)]
    kfm_h = c.sbuf(ps, "kfm_h", [128, KC * 512], BF16)
    kfm_l = c.sbuf(ps, "kfm_l", [128, KC * 512], BF16)
    vfm_h = c.sbuf(ps, "vfm_h", [128, KC * 512], BF16)
    vfm_l = c.sbuf(ps, "vfm_l", [128, KC * 512], BF16)
    eb = [c.sbuf(ps, f"eb{i}", [128, 512], BF16) for i in range(2)]
    ps_tr = [c.psum(ps, f"qt_tr{i}", [128, 1024], BF16) for i in range(2)]
    ps_proj = [c.psum(ps, f"qt_proj{i}", [128, 512], F32) for i in range(2)]

    lnf_t = un_t
    proj_f32 = un_t[:, 0:512]
    phi_e = un_t[:, 512:1024]
    phi_r = un_t[:, 1024:1536]

    dv0, ac0, gp0, pe0 = c.dv.n, c.ac.n, c.gp.n, c.pe.n
    DVQ, ACQ, GPQ, PEQ = 176, 44, 16, 112

    def dvq(q):
        return dv0 + q * DVQ

    def acq(q):
        return ac0 + q * ACQ

    def gpq(q):
        return gp0 + 1 + q * GPQ

    def peq(q):
        return pe0 + q * PEQ

    def ptr_release_dv(U):
        q, u = divmod(U, 64)
        if u < 32:
            cc, bb = divmod(u, 8)
            return dvq(q) + 26 * cc + 18 + bb + 1
        e = u - 32
        return dvq(q) + 144 + e + 1

    qx = [c.q[9], c.q[10]]
    qx_base = [q.n for q in qx]
    q15_base = c.q[15].n
    qwh = [c.q[11], c.q[12]]
    qwl = [c.q[13], c.q[14]]
    qwh_base = [q.n for q in qwh]
    qwl_base = [q.n for q in qwl]

    def lnfm_view(buf, bb, cc):
        v = buf[:].rearrange("p (dc t) -> p dc t", dc=FC)
        return v[:, bb * 8:(bb + 1) * 8, cc * 128:(cc + 1) * 128]

    with nc.Block() as blk:
        def b_sync(E):
            for q in range(4):
                for cc in range(4):
                    ti = q * 4 + cc
                    if ti >= 2:
                        qp, cp = divmod(ti - 2, 4)
                        E.wait(c.ac, acq(qp) + 3 * cp + 3)
                        E.wait(c.dv, dvq(qp) + 26 * cp + 17)
                    r0 = q * 512 + cc * 128
                    E.dma(qx[ti % 2], x0t[ti % 2][:], c.d["x"][r0:r0 + 128, :])
                for mc in range(16):
                    wi = q * 16 + mc
                    if wi >= 2:
                        qp, mp = divmod(wi - 2, 16)
                        E.wait(c.pe, peq(qp) + 32 + mp + 1)
                    if mc < 8:
                        hi_src = c.d["wkT_hi"][mc]
                        lo_src = c.d["wkT_lo"][mc]
                    else:
                        hi_src = c.d["wvT_hi"][mc - 8]
                        lo_src = c.d["wvT_lo"][mc - 8]
                    E.dma(qwh[mc % 2], wtb_h[mc % 2][:], hi_src)
                    E.dma(qwl[mc % 2], wtb_l[mc % 2][:], lo_src)
                if q < 2:
                    E.wait(c.dv, dvq(q) + 104)
                    E.dma(c.q[15], c.d["ln1fm_hi"][q], lnfm_h[:])
                    E.dma(c.q[15], c.d["ln1fm_lo"][q], lnfm_l[:])
        c.sync_body(blk, b_sync)

        def b_pool(E):
            E.memset(t["scol"][:], 0.0)
            for q in range(4):
                for mc in range(8):
                    if q >= 1 and mc == 0:
                        E.wait(c.pe, peq(q - 1) + 112)
                    E.wait(c.dv, dvq(q) + 104 + 4 * mc + 1)
                    E.tensor_copy(out=kfm_h[:, mc * 512:(mc + 1) * 512],
                                  in_=proj_f32)
                for j in range(8):
                    E.wait(c.ac, acq(q) + 36 + j + 1)
                    E.tensor_copy(out=vfm_h[:, j * 512:(j + 1) * 512],
                                  in_=proj_f32)
        c.pool_body(blk, b_pool)

        def b_act(E):
            for q in range(4):
                for cc in range(4):
                    ti = q * 4 + cc
                    E.wait(c.dv, dvq(q) + 26 * cc + 10)
                    E.activation(out=stat[:, 51:52], in_=stat[:, 50:51],
                                 func=AF.Ln)
                    E.activation(out=stat[:, 51:52], in_=stat[:, 51:52],
                                 func=AF.Exp, scale=-0.5)
                    E.wait(qx[ti % 2], qx_base[ti % 2] + 16 * (ti // 2 + 1))
                    E.wait(c.dv, dvq(q) + 26 * cc + 16)
                    if ti >= 1:
                        qp2, cp2 = divmod(ti - 1, 4)
                        E.wait(c.dv, dvq(qp2) + 26 * cp2 + 18)
                        E.wait(c.pe, peq(qp2) + 8 * cp2 + 4)
                    E.activation(out=ln1h_t[:], in_=x0t[ti % 2][:],
                                 func=AF.Identity,
                                 scale=stat[:, 53:54], bias=stat[:, 54:55])
                for mc in range(8):
                    E.wait(c.pe, peq(q) + 32 + mc + 1)
                    if mc > 0:
                        E.wait(c.dv, dvq(q) + 104 + 4 * (mc - 1) + 1)
                    elif q > 0:
                        E.wait(c.dv, dvq(q - 1) + 104 + 4 * 7 + 1)
                    E.activation(out=phi_e, in_=ps_proj[mc % 2][:],
                                 func=AF.Relu, scale=-1.0,
                                 bias=t["nbias_cols"][:, 32 + mc:33 + mc])
                    E.activation(out=phi_e, in_=phi_e, func=AF.Exp, scale=-1.0)
                    E.activation(out=phi_r, in_=ps_proj[mc % 2][:],
                                 func=AF.Relu,
                                 bias=t["bias_cols"][:, 32 + mc:33 + mc])
                for j in range(8):
                    mc = 8 + j
                    E.wait(c.pe, peq(q) + 32 + mc + 1)
                    E.wait(c.gp, gpq(q) + 8 + j)
                    E.wait(c.dv, dvq(q) + 136 + j)
                    E.activation(out=proj_f32, in_=ps_proj[mc % 2][:],
                                 func=AF.Identity,
                                 bias=t["bias_cols"][:, 40 + j:41 + j])
        c.act_body(blk, b_act)

        def b_dve(E):
            for q in range(4):
                for cc in range(4):
                    ti = q * 4 + cc
                    E.wait(qx[ti % 2], qx_base[ti % 2] + 16 * (ti // 2 + 1))
                    if cc == 0 and q > 0:
                        E.wait(c.gp, gpq(q - 1) + 16)
                        if q <= 2:
                            E.wait(c.q[15], q15_base + 32 * q)
                    for sb8 in range(8):
                        E.bn_stats(out=stat[:, 6 * sb8:6 * sb8 + 6],
                                   in_=x0t[ti % 2][:, sb8 * 512:(sb8 + 1) * 512],
                                   dep=(sb8 == 0))
                    E.bn_aggr(out=stat[:, 48:50], in_=stat[:, 0:48])
                    E.tensor_scalar(out=stat[:, 50:51], in0=stat[:, 49:50],
                                    scalar1=1.0, scalar2=EPS_LN,
                                    op0=ALU.mult, op1=ALU.add)
                    E.wait(c.ac, acq(q) + 3 * cc + 2)
                    E.tensor_tensor(out=stat[:, 52:53], in0=stat[:, 51:52],
                                    in1=stat[:, 51:52], op=ALU.mult)
                    E.tensor_tensor(out=stat[:, 52:53], in0=stat[:, 52:53],
                                    in1=stat[:, 50:51], op=ALU.mult)
                    E.tensor_scalar(out=stat[:, 52:53], in0=stat[:, 52:53],
                                    scalar1=-0.5, scalar2=1.5,
                                    op0=ALU.mult, op1=ALU.add)
                    E.tensor_tensor(out=stat[:, 53:54], in0=stat[:, 51:52],
                                    in1=stat[:, 52:53], op=ALU.mult)
                    E.tensor_tensor(out=stat[:, 54:55], in0=stat[:, 48:49],
                                    in1=stat[:, 53:54], op=ALU.mult)
                    E.tensor_scalar_mul(out=stat[:, 54:55], in0=stat[:, 54:55],
                                        scalar1=-1.0)
                    E.tensor_scalar(out=lnf_t[:], in0=x0t[ti % 2][:],
                                    scalar1=stat[:, 53:54],
                                    scalar2=stat[:, 54:55],
                                    op0=ALU.mult, op1=ALU.add)
                    E.wait(c.ac, acq(q) + 3 * cc + 3)
                    if ti >= 1:
                        qp, cp = divmod(ti - 1, 4)
                        E.wait(c.pe, peq(qp) + 8 * cp + 8)
                    E.tensor_tensor(out=ln1l_t[:], in0=lnf_t[:], in1=ln1h_t[:],
                                    op=ALU.subtract)
                    for bb in range(8):
                        E.wait(c.pe, peq(q) + 8 * cc + bb + 1)
                        dsth = lnfm_h if bb < 4 else lnfm_l
                        E.tensor_copy(out=lnfm_view(dsth, bb % 4, cc),
                                      in_=ps_tr[(64 * q + 8 * cc + bb) % 2][:],
                                      dep=False)
                for mc in range(8):
                    E.wait(c.ac, acq(q) + 12 + 3 * mc + 3)
                    if mc > 0:
                        E.wait(c.gp, gpq(q) + mc)
                    elif q > 0:
                        E.wait(c.gp, gpq(q - 1) + 16)
                    else:
                        E.wait(c.gp, gp0 + 1)
                    E.tensor_tensor(out=proj_f32, in0=phi_e, in1=phi_r,
                                    op=ALU.add)
                    E.reduce_sum(out=stat[:, 55:56], in_=proj_f32, axis=AX.X)
                    E.tensor_tensor(out=t["scol"][:, mc:mc + 1],
                                    in0=t["scol"][:, mc:mc + 1],
                                    in1=stat[:, 55:56], op=ALU.add)
                    E.wait(c.gp, gpq(q) + mc + 1)
                    E.tensor_tensor(out=kfm_l[:, mc * 512:(mc + 1) * 512],
                                    in0=proj_f32,
                                    in1=kfm_h[:, mc * 512:(mc + 1) * 512],
                                    op=ALU.subtract)
                for j in range(8):
                    E.wait(c.gp, gpq(q) + 8 + j + 1)
                    E.tensor_tensor(out=vfm_l[:, j * 512:(j + 1) * 512],
                                    in0=proj_f32,
                                    in1=vfm_h[:, j * 512:(j + 1) * 512],
                                    op=ALU.subtract)
                for e in range(32):
                    U = q * 64 + 32 + e
                    E.wait(c.pe, peq(q) + 48 + 2 * e + 1)
                    if e >= 2:
                        E.wait(c.pe, peq(q) + 48 + 2 * (e - 2) + 2)
                    elif q > 0:
                        E.wait(c.pe, peq(q - 1) + 48 + 2 * (30 + e) + 2)
                    E.tensor_copy(out=eb[e % 2][:], in_=ps_tr[U % 2][:, 0:512],
                                  dep=False)
        c.dve_body(blk, b_dve)

        def b_pe(E):
            e = E.e
            kv_started = set()
            for q in range(4):
                for cc in range(4):
                    for bb in range(8):
                        if bb == 0:
                            E.wait(c.ac, acq(q) + 3 * cc + 3)
                        if bb == 4:
                            E.wait(c.dv, dvq(q) + 26 * cc + 18)
                        U = q * 64 + 8 * cc + bb
                        if U >= 2:
                            E.wait(c.dv, ptr_release_dv(U - 2))
                        src = ln1h_t if bb < 4 else ln1l_t
                        for tt2 in range(8):
                            dc = (bb % 4) * 8 + tt2
                            ins = e.transpose(
                                out=ps_tr[U % 2][:, tt2 * 128:(tt2 + 1) * 128],
                                in_=src[:, dc * 128:(dc + 1) * 128],
                                identity=t["id_bf"][:])
                        c.pe.inc(ins)
                E.wait(c.dv, dvq(q) + 104)
                for mc in range(16):
                    wi = q * 16 + mc
                    par = mc % 2
                    E.wait(qwh[par], qwh_base[par] + 16 * (wi // 2 + 1))
                    E.wait(qwl[par], qwl_base[par] + 16 * (wi // 2 + 1))
                    if wi >= 2:
                        qp, mp = divmod(wi - 2, 16)
                        if mp < 8:
                            E.wait(c.ac, acq(qp) + 12 + 3 * mp + 3)
                        else:
                            E.wait(c.ac, acq(qp) + 36 + (mp - 8) + 1)
                    for dc in range(FC):
                        st = (dc == 0)
                        sp = (dc == FC - 1)
                        e.matmul(out=ps_proj[par][:],
                                 lhsT=wtb_h[par][:, dc * 128:(dc + 1) * 128],
                                 rhs=lnfm_h[:, dc * 512:(dc + 1) * 512],
                                 start=st, stop=False)
                        e.matmul(out=ps_proj[par][:],
                                 lhsT=wtb_h[par][:, dc * 128:(dc + 1) * 128],
                                 rhs=lnfm_l[:, dc * 512:(dc + 1) * 512],
                                 start=False, stop=False)
                        ins = e.matmul(out=ps_proj[par][:],
                                       lhsT=wtb_l[par][:, dc * 128:(dc + 1) * 128],
                                       rhs=lnfm_h[:, dc * 512:(dc + 1) * 512],
                                       start=False, stop=sp)
                    c.pe.inc(ins)
                E.wait(c.dv, dvq(q) + 144)
                E.wait(c.gp, gpq(q) + 16)
                for e2 in range(32):
                    kv, cc = divmod(e2, 4)
                    U = q * 64 + 32 + e2
                    E.wait(c.dv, ptr_release_dv(U - 2))
                    for buf, slot in ((kfm_h, 0), (kfm_l, 1), (vfm_h, 2),
                                      (vfm_l, 3)):
                        ins = e.transpose(
                            out=ps_tr[U % 2][:, slot * 128:(slot + 1) * 128],
                            in_=buf[:, kv * 512 + cc * 128:kv * 512 + (cc + 1) * 128],
                            identity=t["id_bf"][:])
                    c.pe.inc(ins)
                    E.wait(c.dv, dvq(q) + 144 + e2 + 1)
                    bank = kv // 4
                    st = bank not in kv_started
                    kv_started.add(bank)
                    out = ps_kv[:, kv * 128:(kv + 1) * 128]
                    e.matmul(out=out, lhsT=eb[e2 % 2][:, 0:128],
                             rhs=eb[e2 % 2][:, 256:384], start=st, stop=False)
                    e.matmul(out=out, lhsT=eb[e2 % 2][:, 0:128],
                             rhs=eb[e2 % 2][:, 384:512], start=False, stop=False)
                    ins = e.matmul(out=out, lhsT=eb[e2 % 2][:, 128:256],
                                   rhs=eb[e2 % 2][:, 256:384], start=False,
                                   stop=(q == 3 and e2 in (15, 31)))
                    c.pe.inc(ins)
        c.pe_body(blk, b_pe)

    assert c.dv.n == dv0 + 4 * DVQ, (c.dv.n, dv0 + 4 * DVQ)
    assert c.ac.n == acq(4), (c.ac.n, acq(4))
    assert c.gp.n == gp0 + 1 + 4 * GPQ
    assert c.pe.n == pe0 + 4 * PEQ
    c._qt_scope = ps



def phase_attn_a(c, s):
    nc = c.nc
    t = c.t
    with contextlib.ExitStack() as ps:
        lnq_h = c.sbuf(ps, f"lnq_h_s{s}", [128, FC * 512], BF16)
        lnq_l = c.sbuf(ps, f"lnq_l_s{s}", [128, FC * 512], BF16)
        attn_h = c.sbuf(ps, f"attn_h_s{s}", [128, FC * 512], BF16)
        attn_l = c.sbuf(ps, f"attn_l_s{s}", [128, FC * 512], BF16)
        wq_h = [c.sbuf(ps, f"aw_h{i}_s{s}", [128, D], BF16) for i in range(3)]
        wq_l = [c.sbuf(ps, f"aw_l{i}_s{s}", [128, D], BF16) for i in range(3)]
        q_f32 = c.sbuf(ps, f"q_f32_s{s}", [128, 512], F32)
        phi_e2 = c.sbuf(ps, f"phi_e2_s{s}", [128, 512], F32)
        phi_r2 = c.sbuf(ps, f"phi_r2_s{s}", [128, 512], F32)
        at_f32 = c.sbuf(ps, f"at_f32_s{s}", [128, 512], F32)
        phiq_h = [c.sbuf(ps, f"phiq_h{i}_s{s}", [128, 512], BF16) for i in range(2)]
        phiq_l = [c.sbuf(ps, f"phiq_l{i}_s{s}", [128, 512], BF16) for i in range(2)]
        o_f32 = c.sbuf(ps, f"o_f32_s{s}", [128, 512], F32)
        o_tr = [c.sbuf(ps, f"o_tr{i}_s{s}", [128, 512], F32) for i in range(2)]
        ps_q = [c.psum(ps, f"a_psq{i}_s{s}", [128, 512], F32) for i in range(3)]
        ps_at = [c.psum(ps, f"a_psat{i}_s{s}", [128, 512], F32) for i in range(3)]
        ps_trf = c.psum(ps, f"a_pstrf_s{s}", [128, 512], F32)

        dv0, ac0, gp0, pe0 = c.dv.n, c.ac.n, c.gp.n, c.pe.n
        q0 = c.q[0]
        q0b = q0.n
        qwh = [c.q[11], c.q[12], c.q[8]]
        qwl = [c.q[13], c.q[14], c.q[9]]
        qwh_base = [q.n for q in qwh]
        qwl_base = [q.n for q in qwl]
        qo = c.qsw[4 * s:4 * s + 4]
        qo_base = [q.n for q in qo]

        def pe_q_at(h):
            return (pe0 + 1) if h == 0 else (pe0 + 2 * h)

        def pe_at_at(h):
            return (pe0 + 64) if h == 31 else (pe0 + 2 * h + 3)

        with nc.Block() as blk:
            def b_sync(E):
                E.dma(q0, lnq_h[:], c.d["ln1fm_hi"][s])
                E.dma(q0, lnq_l[:], c.d["ln1fm_lo"][s])
                for g in range(64):
                    if g >= 3:
                        if g < 35:
                            E.wait(c.pe, pe_at_at(g - 3))
                        else:
                            E.wait(c.pe, pe0 + 64 + 2 * (g - 32 - 2) + 1)
                    if g < 32:
                        hi_src = c.d["wqT_hi"][g]
                        lo_src = c.d["wqT_lo"][g]
                    else:
                        hi_src = c.d["woT_hi"][g - 32]
                        lo_src = c.d["woT_lo"][g - 32]
                    E.dma(qwh[g % 3], wq_h[g % 3][:], hi_src)
                    E.dma(qwl[g % 3], wq_l[g % 3][:], lo_src)
            c.sync_body(blk, b_sync)

            def b_pe(E):
                e = E.e
                E.wait(q0, q0b + 32)

                def emit_q(h):
                    par = h % 3
                    bpar = h % 3
                    E.wait(qwh[bpar], qwh_base[bpar] + 16 * (h // 3 + 1))
                    E.wait(qwl[bpar], qwl_base[bpar] + 16 * (h // 3 + 1))
                    if h >= 3:
                        E.wait(c.ac, ac0 + 4 * (h - 3) + 3)
                    for dc in range(FC):
                        st = (dc == 0)
                        sp = (dc == FC - 1)
                        e.matmul(out=ps_q[par][:],
                                 lhsT=wq_h[bpar][:, dc * 128:(dc + 1) * 128],
                                 rhs=lnq_h[:, dc * 512:(dc + 1) * 512],
                                 start=st, stop=False)
                        e.matmul(out=ps_q[par][:],
                                 lhsT=wq_h[bpar][:, dc * 128:(dc + 1) * 128],
                                 rhs=lnq_l[:, dc * 512:(dc + 1) * 512],
                                 start=False, stop=False)
                        ins = e.matmul(out=ps_q[par][:],
                                       lhsT=wq_l[bpar][:, dc * 128:(dc + 1) * 128],
                                       rhs=lnq_h[:, dc * 512:(dc + 1) * 512],
                                       start=False, stop=sp)
                    c.pe.inc(ins)

                def emit_at(h):
                    par = h % 3
                    bpar = h % 2
                    kv = h // 4
                    E.wait(c.dv, dv0 + 3 * h + 2)
                    E.wait(c.gp, gp0 + 2 * h + 1)
                    if h >= 3:
                        E.wait(c.ac, ac0 + 4 * (h - 3) + 4)
                    kvs = slice(kv * 128, (kv + 1) * 128)
                    e.matmul(out=ps_at[par][:], lhsT=t["kvt_hi"][:, kvs],
                             rhs=phiq_h[bpar][:], start=True, stop=False)
                    e.matmul(out=ps_at[par][:], lhsT=t["kvt_hi"][:, kvs],
                             rhs=phiq_l[bpar][:], start=False, stop=False)
                    ins = e.matmul(out=ps_at[par][:], lhsT=t["kvt_lo"][:, kvs],
                                   rhs=phiq_h[bpar][:], start=False, stop=True)
                    c.pe.inc(ins)

                emit_q(0)
                for h in range(1, 32):
                    emit_q(h)
                    emit_at(h - 1)
                emit_at(31)
                E.wait(c.dv, dv0 + 96)
                E.wait(c.gp, gp0 + 64)
                for mc in range(FC):
                    g = 32 + mc
                    par = (32 + mc) % 3
                    bpar = g % 3
                    E.wait(qwh[bpar], qwh_base[bpar] + 16 * (g // 3 + 1))
                    E.wait(qwl[bpar], qwl_base[bpar] + 16 * (g // 3 + 1))
                    if mc >= 3:
                        E.wait(c.ac, ac0 + 128 + (mc - 3) + 1)
                    else:
                        E.wait(c.ac, ac0 + 4 * (29 + mc) + 3)
                    for fc in range(FC):
                        st = (fc == 0)
                        sp = (fc == FC - 1)
                        e.matmul(out=ps_q[par][:],
                                 lhsT=wq_h[bpar][:, fc * 128:(fc + 1) * 128],
                                 rhs=attn_h[:, fc * 512:(fc + 1) * 512],
                                 start=st, stop=False)
                        e.matmul(out=ps_q[par][:],
                                 lhsT=wq_l[bpar][:, fc * 128:(fc + 1) * 128],
                                 rhs=attn_h[:, fc * 512:(fc + 1) * 512],
                                 start=False, stop=False)
                        ins = e.matmul(out=ps_q[par][:],
                                       lhsT=wq_h[bpar][:, fc * 128:(fc + 1) * 128],
                                       rhs=attn_l[:, fc * 512:(fc + 1) * 512],
                                       start=False, stop=sp)
                    c.pe.inc(ins)
                    E.wait(c.ac, ac0 + 128 + mc + 1)
                    if mc >= 1:
                        E.wait(c.dv, dv0 + 96 + (mc - 1) + 1)
                    for tt2 in range(4):
                        ins = e.transpose(
                            out=ps_trf[:, tt2 * 128:(tt2 + 1) * 128],
                            in_=o_f32[:, tt2 * 128:(tt2 + 1) * 128],
                            identity=t["id_f32"][:])
                    c.pe.inc(ins)
            c.pe_body(blk, b_pe)

            def b_act(E):
                for h in range(32):
                    par = h % 3
                    E.wait(c.pe, pe_q_at(h))
                    if h >= 1:
                        E.wait(c.dv, dv0 + 3 * (h - 1) + 1)
                    E.activation(out=phi_e2[:], in_=ps_q[par][:], func=AF.Relu,
                                 scale=-1.0, bias=t["nbias_cols"][:, h:h + 1])
                    E.activation(out=phi_e2[:], in_=phi_e2[:], func=AF.Exp,
                                 scale=-1.0)
                    E.activation(out=phi_r2[:], in_=ps_q[par][:], func=AF.Relu,
                                 bias=t["bias_cols"][:, h:h + 1])
                    E.wait(c.pe, pe_at_at(h))
                    if h >= 1:
                        E.wait(c.gp, gp0 + 2 * (h - 1) + 2)
                        E.wait(c.dv, dv0 + 3 * (h - 1) + 3)
                    kv = h // 4
                    E.activation(out=at_f32[:], in_=ps_at[par][:],
                                 func=AF.Identity,
                                 scale=t["recip"][:, kv:kv + 1])
                for mc in range(FC):
                    E.wait(c.pe, pe0 + 64 + 2 * mc + 1)
                    E.activation(out=o_f32[:], in_=ps_q[(32 + mc) % 3][:],
                                 func=AF.Identity,
                                 bias=t["bias_cols"][:, 48 + mc:49 + mc])
            c.act_body(blk, b_act)

            def b_dve(E):
                for h in range(32):
                    E.wait(c.ac, ac0 + 4 * h + 3)
                    if h >= 1:
                        E.wait(c.gp, gp0 + 2 * (h - 1) + 1)
                    E.tensor_tensor(out=q_f32[:], in0=phi_e2[:], in1=phi_r2[:],
                                    op=ALU.add)
                    E.wait(c.gp, gp0 + 2 * h + 1)
                    if h >= 2:
                        E.wait(c.pe, pe_at_at(h - 2))
                    E.tensor_tensor(out=phiq_l[h % 2][:], in0=q_f32[:],
                                    in1=phiq_h[h % 2][:], op=ALU.subtract)
                    E.wait(c.gp, gp0 + 2 * h + 2)
                    E.wait(c.ac, ac0 + 4 * h + 4)
                    E.tensor_tensor(out=attn_l[:, h * 512:(h + 1) * 512],
                                    in0=at_f32[:],
                                    in1=attn_h[:, h * 512:(h + 1) * 512],
                                    op=ALU.subtract)
                for mc in range(FC):
                    E.wait(c.pe, pe0 + 64 + 2 * mc + 2)
                    if mc >= 2:
                        for i in range(4):
                            E.wait(qo[i], qo_base[i] + 16 * (mc - 1))
                    E.tensor_copy(out=o_tr[mc % 2][:], in_=ps_trf[:])
            c.dve_body(blk, b_dve)

            def b_pool(E):
                for h in range(32):
                    E.wait(c.dv, dv0 + 3 * h + 1)
                    if h >= 2:
                        E.wait(c.pe, pe_at_at(h - 2))
                    E.tensor_copy(out=phiq_h[h % 2][:], in_=q_f32[:])
                    E.wait(c.ac, ac0 + 4 * h + 4)
                    E.tensor_copy(out=attn_h[:, h * 512:(h + 1) * 512],
                                  in_=at_f32[:])
                tg = s * 4
                for mc in range(FC):
                    E.wait(c.dv, dv0 + 96 + mc + 1)
                    for tt2 in range(4):
                        E.dma(qo[tt2], c.d["o_tm"][tg + tt2, mc],
                              o_tr[mc % 2][:, tt2 * 128:(tt2 + 1) * 128])
            c.pool_body(blk, b_pool)

            if s == 0 and ("dbg_attnh" in DEBUG_TAPS):
                d_ah = c.dram("dbg_attnh", [128, FC * 512], BF16)
                d_al = c.dram("dbg_attnl", [128, FC * 512], BF16)
                d_lq = c.dram("dbg_lnqh", [128, FC * 512], BF16)

                def b_dbg(E):
                    E.wait(c.dv, dv0 + 96)
                    E.wait(c.gp, gp0 + 64)
                    E.dma(c.q[15], d_ah, attn_h[:])
                    E.dma(c.q[15], d_al, attn_l[:])
                    E.dma(c.q[15], d_lq, lnq_h[:])
                c.scalarq_body(blk, b_dbg)

        assert c.pe.n == pe0 + 128
        assert c.ac.n == ac0 + 160
        assert c.dv.n == dv0 + 128
        assert c.gp.n == gp0 + 64



def _ln_recipe_dve(E, c, x_ap, stat2, ac_y0):
    for sb8 in range(8):
        E.bn_stats(out=stat2[:, 6 * sb8:6 * sb8 + 6],
                   in_=x_ap[:, sb8 * 512:(sb8 + 1) * 512], dep=(sb8 == 0))
    E.bn_aggr(out=stat2[:, 48:50], in_=stat2[:, 0:48])
    E.tensor_scalar(out=stat2[:, 50:51], in0=stat2[:, 49:50], scalar1=1.0,
                    scalar2=EPS_LN, op0=ALU.mult, op1=ALU.add)
    E.wait(c.ac, ac_y0)
    E.tensor_tensor(out=stat2[:, 52:53], in0=stat2[:, 51:52],
                    in1=stat2[:, 51:52], op=ALU.mult)
    E.tensor_tensor(out=stat2[:, 52:53], in0=stat2[:, 52:53],
                    in1=stat2[:, 50:51], op=ALU.mult)
    E.tensor_scalar(out=stat2[:, 52:53], in0=stat2[:, 52:53], scalar1=-0.5,
                    scalar2=1.5, op0=ALU.mult, op1=ALU.add)
    E.tensor_tensor(out=stat2[:, 53:54], in0=stat2[:, 51:52],
                    in1=stat2[:, 52:53], op=ALU.mult)
    E.tensor_tensor(out=stat2[:, 54:55], in0=stat2[:, 48:49],
                    in1=stat2[:, 53:54], op=ALU.mult)
    E.tensor_scalar_mul(out=stat2[:, 54:55], in0=stat2[:, 54:55], scalar1=-1.0)


def phase_attn_b(c, s):
    nc = c.nc
    t = c.t
    with contextlib.ExitStack() as ps:
        o_t = [c.sbuf(ps, f"bo_t{i}_s{s}", [128, D], F32) for i in range(2)]
        x0b = [c.sbuf(ps, f"bx0{i}_s{s}", [128, D], F32) for i in range(2)]
        x1_t = c.sbuf(ps, f"bx1_s{s}", [128, D], F32)
        lnf2 = c.sbuf(ps, f"blnf_s{s}", [128, D], F32)
        l3h_t = c.sbuf(ps, f"bl3h_s{s}", [128, D], BF16)
        l3l_t = c.sbuf(ps, f"bl3l_s{s}", [128, D], BF16)
        asm3_h = c.sbuf(ps, f"basm_h_s{s}", [128, D], BF16)
        asm3_l = c.sbuf(ps, f"basm_l_s{s}", [128, D], BF16)
        stat2 = c.sbuf(ps, f"bstat_s{s}", [128, 56], F32)
        ps_tr3 = [c.psum(ps, f"b_tr{i}_s{s}", [128, 1024], BF16) for i in range(2)]

        dv0, ac0, pe0 = c.dv.n, c.ac.n, c.pe.n
        DVT, ACT_T, PET = 44, 5, 8
        qo_ld = [c.q[4], c.q[5]]
        qx_ld = [c.q[6], c.q[7]]
        qst = [c.q[8], c.q[9], c.q[10]]
        qo_b = [q.n for q in qo_ld]
        qx_b = [q.n for q in qx_ld]
        qst_b = [q.n for q in qst]

        def dvt(cc):
            return dv0 + cc * DVT

        def act(cc):
            return ac0 + cc * ACT_T

        def pet(cc):
            return pe0 + cc * PET

        with nc.Block() as blk:
            def b_sync(E):
                for cc in range(4):
                    tg = s * 4 + cc
                    if cc >= 2:
                        E.wait(c.dv, dvt(cc - 2) + 18)
                    E.dma(qo_ld[cc % 2],
                          o_t[cc % 2][:].rearrange("p (m t) -> p m t", m=FC),
                          c.d["o_tm"][tg].rearrange("m p t -> p m t"))
                    E.dma(qx_ld[cc % 2], x0b[cc % 2][:],
                          c.d["x"][tg * 128:(tg + 1) * 128, :])
                    E.wait(c.dv, dvt(cc) + 18)
                    E.dma(qst[0], c.d["x1"][tg * 128:(tg + 1) * 128, :], x1_t[:])
                    E.wait(c.dv, dvt(cc) + 44)
                    E.dma(qst[1],
                          c.d["ln3fm_hi"][s, :, cc],
                          asm3_h[:].rearrange("p (dc t) -> p dc t", dc=FC))
                    E.dma(qst[2],
                          c.d["ln3fm_lo"][s, :, cc],
                          asm3_l[:].rearrange("p (dc t) -> p dc t", dc=FC))
            c.sync_body(blk, b_sync)

            def b_dve(E):
                for cc in range(4):
                    E.wait(qo_ld[cc % 2], qo_b[cc % 2] + 16 * (cc // 2 + 1))
                    if cc >= 1:
                        E.wait(qst[0], qst_b[0] + 16 * cc)
                    _ln_recipe_dve(E, c, o_t[cc % 2][:], stat2, act(cc) + 2)
                    E.tensor_scalar(out=lnf2[:], in0=o_t[cc % 2][:],
                                    scalar1=stat2[:, 53:54],
                                    scalar2=stat2[:, 54:55],
                                    op0=ALU.mult, op1=ALU.add)
                    E.wait(qx_ld[cc % 2], qx_b[cc % 2] + 16 * (cc // 2 + 1))
                    E.tensor_tensor(out=x1_t[:], in0=lnf2[:],
                                    in1=x0b[cc % 2][:], op=ALU.add)
                    _ln_recipe_dve(E, c, x1_t[:], stat2, act(cc) + 4)
                    E.tensor_scalar(out=lnf2[:], in0=x1_t[:],
                                    scalar1=stat2[:, 53:54],
                                    scalar2=stat2[:, 54:55],
                                    op0=ALU.mult, op1=ALU.add)
                    E.wait(c.ac, act(cc) + 5)
                    E.tensor_tensor(out=l3l_t[:], in0=lnf2[:], in1=l3h_t[:],
                                    op=ALU.subtract)
                    for bb in range(8):
                        E.wait(c.pe, pet(cc) + bb + 1)
                        if cc >= 1 and bb <= 1:
                            E.wait(qst[1], qst_b[1] + 16 * cc)
                            E.wait(qst[2], qst_b[2] + 16 * cc)
                        dst = asm3_h if bb < 4 else asm3_l
                        E.tensor_copy(
                            out=dst[:].rearrange("p (g x) -> p g x", g=4)
                            [:, bb % 4, :],
                            in_=ps_tr3[bb % 2][:], dep=False)
            c.dve_body(blk, b_dve)

            def b_act(E):
                for cc in range(4):
                    E.wait(c.dv, dvt(cc) + 10)
                    E.activation(out=stat2[:, 51:52], in_=stat2[:, 50:51],
                                 func=AF.Ln)
                    E.activation(out=stat2[:, 51:52], in_=stat2[:, 51:52],
                                 func=AF.Exp, scale=-0.5)
                    E.wait(c.dv, dvt(cc) + 28)
                    E.activation(out=stat2[:, 51:52], in_=stat2[:, 50:51],
                                 func=AF.Ln)
                    E.activation(out=stat2[:, 51:52], in_=stat2[:, 51:52],
                                 func=AF.Exp, scale=-0.5)
                    E.wait(c.dv, dvt(cc) + 34)
                    if cc >= 1:
                        E.wait(c.dv, dvt(cc - 1) + 36)
                        E.wait(c.pe, pet(cc - 1) + 4)
                    E.activation(out=l3h_t[:], in_=x1_t[:], func=AF.Identity,
                                 scale=stat2[:, 53:54], bias=stat2[:, 54:55])
            c.act_body(blk, b_act)

            def b_pe(E):
                e = E.e
                for cc in range(4):
                    for bb in range(8):
                        if bb == 0:
                            E.wait(c.ac, act(cc) + 5)
                        if bb == 4:
                            E.wait(c.dv, dvt(cc) + 36)
                        gb = cc * 8 + bb
                        if gb >= 2:
                            E.wait(c.dv, dvt((gb - 2) // 8) + 36 +
                                   ((gb - 2) % 8) + 1)
                        src = l3h_t if bb < 4 else l3l_t
                        for tt2 in range(8):
                            dc = (bb % 4) * 8 + tt2
                            ins = e.transpose(
                                out=ps_tr3[gb % 2][:, tt2 * 128:(tt2 + 1) * 128],
                                in_=src[:, dc * 128:(dc + 1) * 128],
                                identity=t["id_bf"][:])
                        c.pe.inc(ins)
            c.pe_body(blk, b_pe)

        assert c.dv.n == dv0 + 4 * DVT
        assert c.ac.n == ac0 + 4 * ACT_T
        assert c.pe.n == pe0 + 4 * PET



def phase_kvfin(c):
    nc = c.nc
    t = c.t
    ps_kv = c._ps_kv
    c._qt_scope.close()
    dv0, gp0 = c.dv.n, c.gp.n
    with contextlib.ExitStack() as ps:
        kvt_f32 = c.sbuf(ps, "kvt_f32", [128, DKV], F32)
        sc2 = c.sbuf(ps, "sc2", [128, NKV], F32)
        sc3 = c.sbuf(ps, "sc3", [128, NKV], F32)
        with nc.Block() as blk:
            def b_dve(E):
                E.tensor_copy(out=kvt_f32[:], in_=ps_kv[:])
                E.tensor_scalar_add(out=sc2[:], in0=t["scol"][:], scalar1=EPS_AT)
                E.reciprocal(out=t["recip"][:], in_=sc2[:])
                E.tensor_tensor(out=sc3[:], in0=sc2[:], in1=t["recip"][:],
                                op=ALU.mult)
                E.tensor_scalar(out=sc3[:], in0=sc3[:], scalar1=-1.0,
                                scalar2=2.0, op0=ALU.mult, op1=ALU.add)
                E.tensor_tensor(out=t["recip"][:], in0=t["recip"][:],
                                in1=sc3[:], op=ALU.mult)
                E.wait(c.gp, gp0 + 1)
                E.tensor_tensor(out=t["kvt_lo"][:], in0=kvt_f32[:],
                                in1=t["kvt_hi"][:], op=ALU.subtract)
            c.dve_body(blk, b_dve)

            def b_pool(E):
                E.wait(c.dv, dv0 + 1)
                E.tensor_copy(out=t["kvt_hi"][:], in_=kvt_f32[:])
            c.pool_body(blk, b_pool)

            if "dbg_kvt" in DEBUG_TAPS or "dbg_stats" in DEBUG_TAPS:
                dbg_kvt = c.dram("dbg_kvt", [128, DKV], F32)
                dbg_st = c.dram("dbg_stats", [128, 2 * NKV + 8], F32)

                def b_sync(E):
                    E.wait(c.dv, c.dv.n)
                    E.dma(c.q[15], dbg_kvt, kvt_f32[:])
                    E.dma(c.q[15], dbg_st[:, 0:NKV], t["scol"][:])
                    E.dma(c.q[15], dbg_st[:, NKV:2 * NKV], t["recip"][:])
                    E.dma(c.q[15], dbg_st[:, 2 * NKV:], t["scale_half"][:])
                c.sync_body(blk, b_sync)
    c._kv_scope.close()



def _emit_tprep(c, blk_scope, which, tT):
    nc = c.nc
    t = c.t
    down = (which == "down")
    wsrc = c.d["w_" + which]
    sidx = {"gate": 0, "up": 1, "down": 2}[which]
    with contextlib.ExitStack() as ps:
        if not down:
            wnat = [c.sbuf(ps, f"tp_w{i}_{which}", [128, 10 * D8], F32)
                    for i in range(2)]
            wn86 = [c.sbuf(ps, f"tp_w86{i}_{which}", [128, D8], F32)
                    for i in range(2)]
            tnat = c.sbuf(ps, f"tp_t_{which}", [128, 11 * D8], BF16)
            msk = c.sbuf(ps, f"tp_m_{which}", [128, 11 * D8], BF16)
            sgn = c.sbuf(ps, f"tp_s_{which}", [128, 11 * D8], BF16)
            absw = c.sbuf(ps, f"tp_a_{which}", [128, 11 * D8], F32)
            NW = 10 * D8
        else:
            wnat = [c.sbuf(ps, f"tp_w{i}_{which}", [128, 4 * H8], F32)
                    for i in range(2)]
            tnat = c.sbuf(ps, f"tp_t_{which}", [128, 4 * H8], BF16)
            msk = c.sbuf(ps, f"tp_m_{which}", [128, 4 * H8], BF16)
            sgn = c.sbuf(ps, f"tp_s_{which}", [128, 4 * H8], BF16)
            absw = c.sbuf(ps, f"tp_a_{which}", [128, 4 * H8], F32)
        ptr = [c.psum(ps, f"tp_ptr{i}_{which}", [128, 2048], BF16)
               for i in range(2)]
        dv0, ac0, gp0, pe0 = c.dv.n, c.ac.n, c.gp.n, c.pe.n
        qa = [c.q[0], c.q[1]]
        qa_b = [q.n for q in qa]
        if not down:
            DVJ, ACJ, PEJ = 4 + 8, 4, 4
        else:
            DVJ, ACJ, PEJ = 2 + 11, 2, 11

        with nc.Block() as blk:
            def b_sync(E):
                for j in range(8):
                    if j >= 2:
                        E.wait(c.dv, dv0 + DVJ * (j - 1))
                    if not down:
                        E.dma(qa[j % 2], wnat[j % 2][:],
                              wsrc[j, 0:1280, :].rearrange(
                                  "(oc p) d -> p oc d", p=128))
                        E.dma(qa[j % 2], wn86[j % 2][0:LAST_OC, :],
                              wsrc[j, 1280:H8, :])
                    else:
                        E.dma(qa[j % 2], wnat[j % 2][:],
                              wsrc[j].rearrange("(oc p) d -> p oc d", p=128))
            c.sync_body(blk, b_sync)

            def b_dve(E):
                for j in range(8):
                    E.wait(qa[j % 2], qa_b[j % 2] + 16 * (2 - down) * (j // 2 + 1))
                    E.wait(c.ac, ac0 + ACJ * j + (2 if not down else 1))
                    if j >= 1:
                        E.wait(c.pe, pe0 + PEJ * j)
                    if not down:
                        E.tensor_single_scalar(
                            out=msk[:, 0:NW], in_=absw[:, 0:NW],
                            scalar=t["scale_half"][:, sidx:sidx + 1],
                            op=ALU.is_gt)
                        E.tensor_single_scalar(
                            out=msk[0:LAST_OC, NW:], in_=absw[0:LAST_OC, NW:],
                            scalar=t["scale_half"][0:LAST_OC, sidx:sidx + 1],
                            op=ALU.is_gt)
                        E.wait(c.ac, ac0 + ACJ * (j + 1))
                        E.tensor_tensor(out=tnat[:, 0:NW], in0=msk[:, 0:NW],
                                        in1=sgn[:, 0:NW], op=ALU.mult)
                        E.tensor_tensor(out=tnat[0:LAST_OC, NW:],
                                        in0=msk[0:LAST_OC, NW:],
                                        in1=sgn[0:LAST_OC, NW:], op=ALU.mult,
                                        dep=False)
                    else:
                        E.tensor_single_scalar(
                            out=msk[:], in_=absw[:],
                            scalar=t["scale_half"][:, sidx:sidx + 1],
                            op=ALU.is_gt)
                        E.wait(c.ac, ac0 + ACJ * (j + 1))
                        E.tensor_tensor(out=tnat[:], in0=msk[:], in1=sgn[:],
                                        op=ALU.mult)
                    nev = 8 if not down else 11
                    for ev in range(nev):
                        if not down:
                            dcc, half = divmod(ev, 2)
                            E.wait(c.pe, pe0 + PEJ * j + dcc + 1)
                            base = (j * 4 + dcc) * H8
                            if half == 0:
                                E.tensor_copy(out=tT[:, base:base + 1024],
                                              in_=ptr[(j * 4 + dcc) % 2][:, 0:1024],
                                              dep=False)
                            else:
                                E.tensor_copy(out=tT[:, base + 1024:base + H8],
                                              in_=ptr[(j * 4 + dcc) % 2][:, 1024:1366],
                                              dep=False)
                        else:
                            hc = ev
                            E.wait(c.pe, pe0 + PEJ * j + hc + 1)
                            base = (j * 11 + hc) * D8
                            hp = 128 if hc < 10 else LAST_OC
                            E.tensor_copy(out=tT[0:hp, base:base + D8],
                                          in_=ptr[(j * 11 + hc) % 2][0:hp, 0:D8],
                                          dep=False)
            c.dve_body(blk, b_dve)

            def b_act(E):
                for j in range(8):
                    E.wait(qa[j % 2], qa_b[j % 2] + 16 * (2 - down) * (j // 2 + 1))
                    if j >= 1:
                        E.wait(c.dv, dv0 + DVJ * (j - 1) + (4 if not down else 2))
                    if not down:
                        E.activation(out=absw[:, 0:NW], in_=wnat[j % 2][:],
                                     func=AF.Abs)
                        E.activation(out=absw[0:LAST_OC, NW:],
                                     in_=wn86[j % 2][0:LAST_OC, :], func=AF.Abs)
                        E.sign(out=sgn[:, 0:NW], in_=wnat[j % 2][:])
                        E.sign(out=sgn[0:LAST_OC, NW:], in_=wn86[j % 2][0:LAST_OC, :])
                    else:
                        E.activation(out=absw[:], in_=wnat[j % 2][:], func=AF.Abs)
                        E.sign(out=sgn[:], in_=wnat[j % 2][:])
            c.act_body(blk, b_act)


            def b_pe(E):
                e = E.e
                for j in range(8):
                    E.wait(c.dv, dv0 + DVJ * j + (4 if not down else 2))
                    if not down:
                        for dcc in range(4):
                            gidx = j * 4 + dcc
                            if gidx >= 2:
                                E.wait(c.dv, dv0 + DVJ * ((gidx - 2) // 4) + 4 +
                                       2 * ((gidx - 2) % 4) + 2)
                            for oc in range(11):
                                if oc < 10:
                                    in_ap = tnat[:, oc * D8 + dcc * 128:
                                                 oc * D8 + (dcc + 1) * 128]
                                else:
                                    in_ap = tnat[0:LAST_OC,
                                                 NW + dcc * 128:NW + (dcc + 1) * 128]
                                w = 128 if oc < 10 else LAST_OC
                                idw = t["id_bf"][:] if oc < 10 else \
                                    t["id_bf"][0:LAST_OC, 0:LAST_OC]
                                ins = e.transpose(
                                    out=ptr[gidx % 2][:, oc * 128:oc * 128 + w],
                                    in_=in_ap, identity=idw)
                            c.pe.inc(ins)
                    else:
                        for hc in range(11):
                            gidx = j * 11 + hc
                            if gidx >= 2:
                                E.wait(c.dv, dv0 + DVJ * ((gidx - 2) // 11) + 2 +
                                       ((gidx - 2) % 11) + 1)
                            hp = 128 if hc < 10 else LAST_OC
                            for oc in range(4):
                                in_ap = tnat[:, oc * H8 + hc * 128:
                                             oc * H8 + hc * 128 + hp]
                                ins = e.transpose(
                                    out=ptr[gidx % 2][0:hp, oc * 128:(oc + 1) * 128],
                                    in_=in_ap, identity=t["id_bf"][:])
                            c.pe.inc(ins)
            c.pe_body(blk, b_pe)


def _gate_sig(k):
    items = [(i, JOF[i, k], SGN[i, k]) for i in range(8)]
    last_pos = max(n for n, it in enumerate(items) if it[2] > 0)
    last_neg = max(n for n, it in enumerate(items) if it[2] < 0)
    return items, last_pos, last_neg


def phase_ffn_gateup(c, which):
    nc = c.nc
    t = c.t
    outer = contextlib.ExitStack()
    tT = c.sbuf(outer, f"tT_{which}", [128, 8 * 4 * H8], BF16)
    _emit_tprep(c, outer, which, tT)
    up = (which == "up")
    with contextlib.ExitStack() as ps:
        ln3h = c.sbuf(ps, f"g_ln3h_{which}", [128, FC * 512], BF16)
        ln3l = c.sbuf(ps, f"g_ln3l_{which}", [128, FC * 512], BF16)
        ev = [c.sbuf(ps, f"g_ev{i}_{which}", [128, 512], F32) for i in range(2)]
        pre = [c.sbuf(ps, f"g_pre{i}_{which}", [128, 512], F32) for i in range(2)]
        res = [c.sbuf(ps, f"g_res{i}_{which}", [128, 512], F32) for i in range(2)]
        if up:
            sgld = [c.sbuf(ps, f"g_sg{i}", [128, 512], F32) for i in range(2)]
            hh = [c.sbuf(ps, f"g_hh{i}", [128, 512], BF16) for i in range(2)]
            hl = [c.sbuf(ps, f"g_hl{i}", [128, 512], BF16) for i in range(2)]
        pp = [c.psum(ps, f"g_pp{i}_{which}", [128, 512], F32) for i in range(2)]
        pn = [c.psum(ps, f"g_pn{i}_{which}", [128, 512], F32) for i in range(2)]

        q0 = c.q[0]
        qsg = [c.q[2], c.q[3]]
        qst = [c.q[4], c.q[5], c.q[6], c.q[7]]
        with nc.Block() as blk:
            dv0, ac0, gp0, pe0 = c.dv.n, c.ac.n, c.gp.n, c.pe.n
            q0b = q0.n
            qsg_b = [q.n for q in qsg]
            qst_b = [q.n for q in qst]
            ACG = 2
            DVG = 2 if up else 1
            GPG = 1 if up else 0

            def b_sync(E):
                for s in range(2):
                    if s:
                        E.wait(c.pe, pe0 + 88)
                    E.dma(q0, ln3h[:], c.d["ln3fm_hi"][s].rearrange(
                        "p a b t -> p (a b t)"))
                    E.dma(q0, ln3l[:], c.d["ln3fm_lo"][s].rearrange(
                        "p a b t -> p (a b t)"))
            c.sync_body(blk, b_sync)

            def b_scalar(E):
                for s in range(2):
                    for g in range(88):
                        k, oc = divmod(g, 11)
                        gg = s * 88 + g
                        par = gg % 2
                        w = 128 if oc < 10 else LAST_OC
                        E.wait(c.pe, pe0 + gg + 1)
                        if up:
                            E.dma(qsg[par], sgld[par][0:w, :],
                                  c.d["sg"][s, g, 0:w, :])
                        E.activation(out=ev[par][0:w, :], in_=pp[par][0:w, :],
                                     func=AF.Copy)
                        E.wait(c.dv, dv0 + DVG * gg + 1)
                        if not up:
                            if gg >= 2:
                                E.wait(qst[par], qst_b[par] + 16 * (gg // 2))
                            E.activation(out=res[par][0:w, :],
                                         in_=pre[par][0:w, :], func=AF.Silu,
                                         scale=t["scale_half"][0:w, 4:5])
                            E.wait(c.ac, c.ac.n)
                            E.dma(qst[par], c.d["sg"][s, g, 0:w, :],
                                  res[par][0:w, :])
                        else:
                            E.wait(c.gp, gp0 + GPG * gg + 1)
                            if gg >= 2:
                                E.wait(qst[2 * par],
                                       qst_b[2 * par] + 16 * (gg // 2))
                            E.activation(out=hh[par][0:w, :],
                                         in_=res[par][0:w, :], func=AF.Identity,
                                         scale=t["scale_half"][0:w, 5:6])
                            E.wait(c.dv, dv0 + DVG * gg + 2)
                            E.dma(qst[2 * par], c.d["h_hi"][s, g, 0:w, :],
                                  hh[par][0:w, :])
                            E.dma(qst[2 * par + 1], c.d["h_lo"][s, g, 0:w, :],
                                  hl[par][0:w, :])
            c.act_body(blk, b_scalar)

            def b_dve(E):
                for s in range(2):
                    for g in range(88):
                        gg = s * 88 + g
                        par = gg % 2
                        oc = g % 11
                        w = 128 if oc < 10 else LAST_OC
                        E.wait(c.ac, ac0 + ACG * gg + 1)
                        E.tensor_tensor(out=pre[par][0:w, :], 
                                        in0=ev[par][0:w, :],
                                        in1=pn[par][0:w, :], op=ALU.subtract)
                        if up:
                            E.wait(c.gp, gp0 + gg + 1)
                            E.wait(c.ac, ac0 + ACG * gg + 2)
                            if gg >= 2:
                                E.wait(qst[2 * par + 1],
                                       qst_b[2 * par + 1] + 16 * (gg // 2))
                            E.scalar_tensor_tensor(
                                out=hl[par][0:w, :], in0=res[par][0:w, :],
                                scalar=t["scale_half"][0:w, 5:6],
                                in1=hh[par][0:w, :],
                                op0=ALU.mult, op1=ALU.subtract)
            c.dve_body(blk, b_dve)

            if up:
                def b_pool(E):
                    for s in range(2):
                        for g in range(88):
                            gg = s * 88 + g
                            par = gg % 2
                            oc = g % 11
                            w = 128 if oc < 10 else LAST_OC
                            E.wait(c.dv, dv0 + DVG * gg + 1)
                            E.wait(qsg[par],
                                   qsg_b[par] + 16 * (gg // 2 + 1))
                            if gg >= 2:
                                E.wait(c.ac, ac0 + ACG * (gg - 2) + 2)
                            E.tensor_tensor(out=res[par][0:w, :],
                                            in0=pre[par][0:w, :],
                                            in1=sgld[par][0:w, :],
                                            op=ALU.mult)
                c.pool_body(blk, b_pool)

            def b_pe(E):
                e = E.e
                for s in range(2):
                    E.wait(q0, q0b + 32 * (s + 1))
                    for g in range(88):
                        k, oc = divmod(g, 11)
                        gg = s * 88 + g
                        par = gg % 2
                        w = 128 if oc < 10 else LAST_OC
                        if gg >= 2:
                            E.wait(c.ac, ac0 + ACG * (gg - 2) + 1)
                            E.wait(c.dv, dv0 + DVG * (gg - 2) + 1)
                        items, last_pos, last_neg = _gate_sig(k)
                        stp = {1: False, -1: False}
                        for n, (i, j, sg_) in enumerate(items):
                            dst = pp[par] if sg_ > 0 else pn[par]
                            first = not stp[sg_]
                            stp[sg_] = True
                            last = (n == (last_pos if sg_ > 0 else last_neg))
                            for dcc in range(4):
                                lhsT = tT[:, (j * 4 + dcc) * H8 + oc * 128:
                                          (j * 4 + dcc) * H8 + oc * 128 + w]
                                dc = i * 4 + dcc
                                rh = ln3h[:].rearrange(
                                    "p (a b t) -> p a b t", a=4, b=FC)[:, :, dc, :]
                                rl = ln3l[:].rearrange(
                                    "p (a b t) -> p a b t", a=4, b=FC)[:, :, dc, :]
                                e.matmul(out=dst[0:w, :], lhsT=lhsT, rhs=rh,
                                         start=(first and dcc == 0),
                                         stop=False)
                                ins = e.matmul(out=dst[0:w, :], lhsT=lhsT,
                                               rhs=rl, start=False,
                                               stop=(last and dcc == 3))
                        c.pe.inc(ins)
            c.pe_body(blk, b_pe)
    outer.close()


def phase_ffn_down(c):
    nc = c.nc
    t = c.t
    outer = contextlib.ExitStack()
    tT = c.sbuf(outer, "tT_down", [128, 8 * 11 * D8], BF16)
    _emit_tprep(c, outer, "down", tT)
    OCB = [list(range(b, min(b + 3, FC))) for b in range(0, FC, 3)]
    with contextlib.ExitStack() as ps:
        hld_h = [c.sbuf(ps, f"d_hh{i}", [128, 11 * 512], BF16) for i in range(4)]
        hld_l = [c.sbuf(ps, f"d_hl{i}", [128, 11 * 512], BF16) for i in range(4)]
        ev = c.sbuf(ps, "d_ev", [128, 512], F32)
        pre = c.sbuf(ps, "d_pre", [128, 512], F32)
        f_tr = [c.sbuf(ps, f"d_tr{i}", [128, 512], F32) for i in range(2)]
        pp = [c.psum(ps, f"d_pp{i}", [128, 512], F32) for i in range(3)]
        pn = [c.psum(ps, f"d_pn{i}", [128, 512], F32) for i in range(3)]
        ps_trf = c.psum(ps, "d_ptrf", [128, 512], F32)

        qh = [c.q[2], c.q[3], c.q[4], c.q[5]]
        qst = c.qsw[8:12]
        with nc.Block() as blk:
            dv0, ac0, pe0 = c.dv.n, c.ac.n, c.pe.n
            qh_b = [q.n for q in qh]
            qst_b = [q.n for q in qst]
            NB = len(OCB)
            def pe_blk(si, b):
                return pe0 + si * (8 * NB + FC) + b * 8 + sum(
                    len(OCB[x]) for x in range(b))

            pe_sched = []
            n = 0
            for si in range(2):
                for b in range(NB):
                    n += 8
                    n += len(OCB[b])
                    pe_sched.append(n)

            def pe_after(si, b):
                return pe0 + pe_sched[si * NB + b]

            def pe_after_loads(si, b):
                return pe0 + pe_sched[si * NB + b] - len(OCB[b])

            def b_sync(E):
                for si in range(2):
                    for b in range(NB):
                        for i in range(8):
                            gi = (si * NB + b) * 8 + i
                            par = gi % 4
                            if gi >= 4:
                                pb, pi = divmod(gi - 4, 8)
                                E.wait(c.pe, pe0 + (pe_sched[pb - 1] if pb else 0)
                                       + pi + 1)
                            E.dma(qh[par],
                                  hld_h[par][:].rearrange("p (g t) -> p g t",
                                                          g=11),
                                  c.d["h_hi"][si].rearrange(
                                      "g p t -> p g t")[:, i * 11:(i + 1) * 11, :])
                            E.dma(qh[par],
                                  hld_l[par][:].rearrange("p (g t) -> p g t",
                                                          g=11),
                                  c.d["h_lo"][si].rearrange(
                                      "g p t -> p g t")[:, i * 11:(i + 1) * 11, :])
            c.sync_body(blk, b_sync)

            def b_pe(E):
                e = E.e
                evac_n = 0
                for si in range(2):
                    for b in range(NB):
                        blkbase = pe0 + (pe_sched[si * NB + b - 1]
                                         if (si * NB + b) else 0)
                        started = {}
                        for i in range(8):
                            gi = (si * NB + b) * 8 + i
                            par = gi % 4
                            E.wait(qh[par], qh_b[par] + 32 * (gi // 4 + 1))
                            if i == 0:
                                if si * NB + b >= 1:
                                    E.wait(c.ac, ac0 + evac_n)
                                    E.wait(c.dv, dv0 + 2 * evac_n)
                            for hc in range(11):
                                hp = 128 if hc < 10 else LAST_OC
                                for gsl, ocd in enumerate(OCB[b]):
                                    k, olocal = divmod(ocd, 4)
                                    j = JOF[i, k]
                                    sg_ = SGN[i, k]
                                    dst = pp[gsl] if sg_ > 0 else pn[gsl]
                                    key = (gsl, sg_ > 0)
                                    first = key not in started
                                    started[key] = True
                                    lhsT = tT[0:hp, (j * 11 + hc) * D8 +
                                              olocal * 128:
                                              (j * 11 + hc) * D8 +
                                              olocal * 128 + 128]
                                    rh = hld_h[par][0:hp, hc * 512:(hc + 1) * 512]
                                    rl = hld_l[par][0:hp, hc * 512:(hc + 1) * 512]
                                    li = max(i2 for i2 in range(8)
                                             if (SGN[i2, k] > 0) == (sg_ > 0))
                                    last = (i == li and hc == 10)
                                    e.matmul(out=dst[:], lhsT=lhsT, rhs=rh,
                                             start=first, stop=False)
                                    ins = e.matmul(out=dst[:], lhsT=lhsT,
                                                   rhs=rl, start=False,
                                                   stop=last)
                            c.pe.inc(ins)
                        for gsl, ocd in enumerate(OCB[b]):
                            E.wait(c.ac, ac0 + evac_n + 1)
                            E.wait(c.dv, dv0 + 2 * evac_n + 1)
                            if evac_n >= 1:
                                E.wait(c.dv, dv0 + 2 * (evac_n - 1) + 2)
                            for tt2 in range(4):
                                ins = e.transpose(
                                    out=ps_trf[:, tt2 * 128:(tt2 + 1) * 128],
                                    in_=pre[:, tt2 * 128:(tt2 + 1) * 128],
                                    identity=t["id_f32"][:])
                            c.pe.inc(ins)
                            evac_n += 1
            c.pe_body(blk, b_pe)

            def b_act(E):
                evac_n = 0
                for si in range(2):
                    for b in range(NB):
                        blkbase = pe0 + (pe_sched[si * NB + b - 1]
                                         if (si * NB + b) else 0)
                        for gsl, ocd in enumerate(OCB[b]):
                            E.wait(c.pe, blkbase + 8)
                            if evac_n >= 1:
                                E.wait(c.dv, dv0 + 2 * (evac_n - 1) + 1)
                            E.activation(out=ev[:], in_=pp[gsl][:],
                                         func=AF.Copy)
                            evac_n += 1
            c.act_body(blk, b_act)

            def b_pool2(E):
                evac_n = 0
                for si in range(2):
                    for b in range(NB):
                        for gsl, ocd in enumerate(OCB[b]):
                            E.wait(c.dv, dv0 + 2 * evac_n + 2)
                            for tt2 in range(4):
                                E.dma(qst[tt2],
                                      c.d["ffn_tm"][si * 4 + tt2, ocd],
                                      f_tr[evac_n % 2][:, tt2 * 128:
                                                       (tt2 + 1) * 128])
                            evac_n += 1
            c.pool_body(blk, b_pool2)

            def b_dve(E):
                evac_n = 0
                for si in range(2):
                    for b in range(NB):
                        blkbase = pe0 + (pe_sched[si * NB + b - 1]
                                         if (si * NB + b) else 0)
                        for gsl, ocd in enumerate(OCB[b]):
                            E.wait(c.ac, ac0 + evac_n + 1)
                            E.tensor_tensor(out=pre[:], in0=ev[:],
                                            in1=pn[gsl][:], op=ALU.subtract)
                            E.wait(c.pe, blkbase + 8 + gsl + 1)
                            if evac_n >= 2:
                                for x in range(4):
                                    E.wait(qst[x], qst_b[x] + 16 * (evac_n - 1))
                            E.tensor_scalar_mul(
                                out=f_tr[evac_n % 2][:], in0=ps_trf[:],
                                scalar1=t["scale_half"][:, 6:7])
                            evac_n += 1
            c.dve_body(blk, b_dve)
    outer.close()



def phase_out(c):
    nc = c.nc
    t = c.t
    with contextlib.ExitStack() as ps:
        f_t = [c.sbuf(ps, f"o_f{i}", [128, D], F32) for i in range(2)]
        x1b = [c.sbuf(ps, f"o_x1{i}", [128, D], F32) for i in range(2)]
        x2_t = [c.sbuf(ps, f"o_x2{i}", [128, D], F32) for i in range(2)]
        lnf4 = c.sbuf(ps, "o_lnf", [128, D], F32)
        stat4 = c.sbuf(ps, "o_stat", [128, 56], F32)
        qf = [c.q[0], c.q[1]]
        qx = [c.q[2], c.q[3]]
        qout = [c.q[4], c.q[5]]
        with nc.Block() as blk:
            dv0, ac0 = c.dv.n, c.ac.n
            qf_b = [q.n for q in qf]
            qx_b = [q.n for q in qx]
            qout_b = [q.n for q in qout]
            DVT = 19
            DVT = 18
            ACT_T = 2

            def b_sync(E):
                for tg in range(8):
                    par = tg % 2
                    if tg >= 2:
                        E.wait(c.dv, dv0 + DVT * (tg - 2) + 18)
                    E.dma(qf[par],
                          f_t[par][:].rearrange("p (m t) -> p m t", m=FC),
                          c.d["ffn_tm"][tg].rearrange("m p t -> p m t"))
                    E.dma(qx[par], x1b[par][:],
                          c.d["x1"][tg * 128:(tg + 1) * 128, :])
                    E.wait(c.dv, dv0 + DVT * tg + 18)
                    E.dma(qout[par], c.d["out"][tg * 128:(tg + 1) * 128, :],
                          x2_t[par][:])
                for q in (qout[0], qout[1]):
                    E.wait(q, q.n)
            c.sync_body(blk, b_sync)

            def b_dve(E):
                for tg in range(8):
                    par = tg % 2
                    E.wait(qf[par], qf_b[par] + 16 * (tg // 2 + 1))
                    _ln_recipe_dve(E, c, f_t[par][:], stat4, ac0 + ACT_T * tg + 2)
                    E.tensor_scalar(out=lnf4[:], in0=f_t[par][:],
                                    scalar1=stat4[:, 53:54],
                                    scalar2=stat4[:, 54:55],
                                    op0=ALU.mult, op1=ALU.add)
                    E.wait(qx[par], qx_b[par] + 16 * (tg // 2 + 1))
                    if tg >= 2:
                        E.wait(qout[par], qout_b[par] + 16 * (tg // 2))
                    E.tensor_tensor(out=x2_t[par][:], in0=lnf4[:],
                                    in1=x1b[par][:], op=ALU.add)
            c.dve_body(blk, b_dve)

            def b_act(E):
                for tg in range(8):
                    E.wait(c.dv, dv0 + DVT * tg + 10)
                    E.activation(out=stat4[:, 51:52], in_=stat4[:, 50:51],
                                 func=AF.Ln)
                    E.activation(out=stat4[:, 51:52], in_=stat4[:, 51:52],
                                 func=AF.Exp, scale=-0.5)
            c.act_body(blk, b_act)


def build_program():
    nc = bass.Bass("TRN2", target_bir_lowering=False)
    es = contextlib.ExitStack()
    c = Ctx(nc, es)
    for s in ("pe", "dv", "ac", "gp", "bar", "dv2"):
        c.sem(s)
    c.q = [c.sem(f"q{i}") for i in range(16)]
    c.qsw = [c.sem(f"qsw{i}") for i in range(12)]
    declare_io(c)

    phases = [phase_init, phase_wprep, phase_quarters, phase_kvfin,
              lambda c: phase_attn_a(c, 0), lambda c: phase_attn_b(c, 0),
              lambda c: phase_attn_a(c, 1), lambda c: phase_attn_b(c, 1),
              lambda c: phase_ffn_gateup(c, "gate"),
              lambda c: phase_ffn_gateup(c, "up"),
              phase_ffn_down, phase_out]
    for i, ph in enumerate(phases):
        if i >= N_PHASES:
            break
        ph(c)
        barrier(c)
    with nc.Block() as blk:
        def b_flush(e):
            for q in c.q + c.qsw:
                if q.n:
                    e.wait_ge(q.h, q.n)
        blk.sync(b_flush)
    es.close()
    return nc


def kernel(**inputs):
    x = np.ascontiguousarray(inputs["L0_attn_hidden_states"])
    ident = np.eye(128, dtype=np.float32)
    base = {
        "Wq": inputs["L0_attn_Wq"], "Wk": inputs["L0_attn_Wk"],
        "Wv": inputs["L0_attn_Wv"], "Wo": inputs["L0_attn_Wo"],
        "bq": inputs["L0_attn_bq"], "bk": inputs["L0_attn_bk"],
        "bv": inputs["L0_attn_bv"], "bo": inputs["L0_attn_bo"],
        "w_gate": inputs["L0_ffn_w_gate"], "w_up": inputs["L0_ffn_w_up"],
        "w_down": inputs["L0_ffn_w_down"], "ident": ident,
    }
    base = {k: np.ascontiguousarray(v) for k, v in base.items()}
    in_maps = []
    for core in range(N_CORES):
        b, hf = core // 2, core % 2
        own = x[b, hf * 1024:(hf + 1) * 1024]
        oth = x[b, (1 - hf) * 1024:(2 - hf) * 1024]
        m = dict(base)
        m["x"] = np.concatenate([own, oth], axis=0)
        in_maps.append(m)
    nc = build_program()
    res = run_bass_kernel_spmd(nc, in_maps, core_ids=list(range(N_CORES)))
    y = np.empty((4, SEQ, D), np.float32)
    for core in range(N_CORES):
        b, hf = core // 2, core % 2
        y[b, hf * 1024:(hf + 1) * 1024] = res.results[core]["out"]
    return y
